# version 21
# baseline (speedup 1.0000x reference)
import math
import contextlib
import numpy as np

import concourse.bass as bass
import concourse.mybir as mybir
from concourse.bass_utils import run_bass_kernel_spmd

F32 = mybir.dt.float32
BF16 = mybir.dt.bfloat16
I32 = mybir.dt.int32
AF = mybir.ActivationFunctionType
OP = mybir.AluOpType
AX = mybir.AxisListType

NCORES = 8
P = 128

REAL_CFG = dict(D=2048, H=16, CH=512, HID=8192)
SHRINKS = [float(np.float32(s)) for s in np.linspace(0.4, 1.0, 16)]
CLIP_C = 15.25
QMAX = 15.0
SQRT_DK_INV = float(np.float32(1.0 / math.sqrt(128.0)))
LN_EPS = 1e-5


def build_program(cfg):
    D = cfg["D"]; H = cfg["H"]; CH = cfg["CH"]; HID = cfg["HID"]
    DK = 128
    assert D == H * DK and CH % P == 0
    TLOC = 2 * CH
    NTT = TLOC // P
    DSL = D // NCORES
    HIDC = HID // NCORES
    NIC = D // P
    NKSUB = CH // P
    TW = min(512, TLOC)
    DW = min(512, D)
    NTW = TLOC // TW
    NDW = D // DW
    assert DSL % P == 0 or DSL < P
    QFD_FC = D
    QFD_WP = HIDC
    QFDM = max(QFD_FC, QFD_WP)
    NQT_FC = HIDC // P
    NQT_WP = D // P
    NBQ = QFDM // 64
    FFN_SLAB = min(2048, HID)
    NSLAB = HID // FFN_SLAB
    NHC_SLAB = FFN_SLAB // P
    assert H >= NHC_SLAB or True
    NCH2 = max(H, NHC_SLAB)

    nc = bass.Bass("TRN2", target_bir_lowering=False, debug=False)

    x0 = nc.dram_tensor("x0", [CH, D], F32, kind="ExternalInput").ap()
    x1 = nc.dram_tensor("x1", [CH, D], F32, kind="ExternalInput").ap()
    wq_d = nc.dram_tensor("wq", [DSL, D], F32, kind="ExternalInput").ap()
    wk_d = nc.dram_tensor("wk", [DSL, D], F32, kind="ExternalInput").ap()
    wv_d = nc.dram_tensor("wv", [DSL, D], F32, kind="ExternalInput").ap()
    wo_d = nc.dram_tensor("wo", [DSL, D], F32, kind="ExternalInput").ap()
    wfc_d = nc.dram_tensor("wfc", [HIDC, D], F32, kind="ExternalInput").ap()
    wp_d = nc.dram_tensor("wp", [D, HIDC], F32, kind="ExternalInput").ap()
    out0 = nc.dram_tensor("out0", [CH, D], F32, kind="ExternalOutput").ap()
    out1 = nc.dram_tensor("out1", [CH, D], F32, kind="ExternalOutput").ap()

    agw_in = nc.dram_tensor("agw_in", [4 * D, DSL], BF16)
    agw_out = nc.dram_tensor("agw_out", [NCORES * 4 * D, DSL], BF16)
    agkt_in = nc.dram_tensor("agkt_in", [D, TLOC], BF16)
    agkt_out = nc.dram_tensor("agkt_out", [NCORES * D, TLOC], BF16)
    agv_in = nc.dram_tensor("agv_in", [TLOC, D], BF16)
    agv_out = nc.dram_tensor("agv_out", [NCORES * TLOC, D], BF16)
    agfc_in = nc.dram_tensor("agfc_in", [D, HIDC], BF16)
    agfc_out = nc.dram_tensor("agfc_out", [NCORES * D, HIDC], BF16)
    agwp_in = nc.dram_tensor("agwp_in", [HIDC, D], BF16)
    agwp_out = nc.dram_tensor("agwp_out", [NCORES * HIDC, D], BF16)
    if cfg.get("debug_deq"):
        dbg_fc = nc.dram_tensor("dbg_fc", [D, HIDC], BF16, kind="ExternalOutput").ap()
        dbg_wp = nc.dram_tensor("dbg_wp", [HIDC, D], BF16, kind="ExternalOutput").ap()

    ctr = {}

    def _inc(key, n):
        ctr[key] = ctr.get(key, 0) + n
        return ctr[key]

    def _val(key):
        return ctr.get(key, 0)

    with contextlib.ExitStack() as stk:
        ec = stk.enter_context

        ARN = max(8 * QFDM, NTT * D)
        arena = ec(nc.sbuf_tensor("arena", [P, ARN], F32))
        xnT = ec(nc.sbuf_tensor("xnT", [P, NIC, TLOC], BF16))
        QT = ec(nc.sbuf_tensor("QT", [P, NCH2, TLOC], BF16))
        Abuf = ec(nc.sbuf_tensor("Abuf", [P, max(H, NHC_SLAB), max(TLOC, 2 * DW)], BF16))
        xbuf = ec(nc.sbuf_tensor("xbuf", [P, max(D, max(1, min(4, NTT // 2)) * DW)], F32))
        xn = None
        wbig = ec(nc.sbuf_tensor("wbig", [P, 2, D], BF16))
        kvstage = None
        kbuf = ec(nc.sbuf_tensor("kbuf", [P, 2, CH], BF16))
        vbuf = ec(nc.sbuf_tensor("vbuf", [P, 2, CH], BF16))
        pbuf = ec(nc.sbuf_tensor("pbuf", [P, 1, max(NKSUB * CH, 4 * TW)], BF16))
        tstage = ec(nc.sbuf_tensor("tstage", [P, 4, P], BF16))
        dmask = ec(nc.sbuf_tensor("dmask", [P, NKSUB, CH], BF16))
        ocol = ec(nc.sbuf_tensor("ocol", [P, H, P], BF16))
        selr = ec(nc.sbuf_tensor("selr", [H, H, P], BF16))
        onesH = ec(nc.sbuf_tensor("onesH", [H, P], F32))
        rowidH = ec(nc.sbuf_tensor("rowidH", [H, 1], I32))
        rowidHf = ec(nc.sbuf_tensor("rowidHf", [H, 1], F32))
        rowtmpH = ec(nc.sbuf_tensor("rowtmpH", [H, 1], F32))
        rcb = ec(nc.sbuf_tensor("rcb", [H, CH], BF16))
        bcb = ec(nc.sbuf_tensor("bcb", [P, CH], F32))
        relub = None
        amax = ec(nc.sbuf_tensor("amax", [P, NBQ], F32))
        qbase = ec(nc.sbuf_tensor("qbase", [P, NBQ], F32))
        qinvb = ec(nc.sbuf_tensor("qinvb", [P, NBQ], F32))
        qsse = ec(nc.sbuf_tensor("qsse", [P, NBQ], F32))
        qt1 = ec(nc.sbuf_tensor("qt1", [P, NBQ], F32))
        qbest = ec(nc.sbuf_tensor("qbest", [P, NBQ], F32))
        qmaskb = ec(nc.sbuf_tensor("qmaskb", [P, NBQ], F32))
        qbsc = ec(nc.sbuf_tensor("qbsc", [P, NBQ], F32))
        qscb = ec(nc.sbuf_tensor("qscb", [P, NBQ], F32))
        qscb2 = ec(nc.sbuf_tensor("qscb2", [P, NBQ], F32))
        qrb = ec(nc.sbuf_tensor("qrb", [P, NBQ], F32))
        qbh = ec(nc.sbuf_tensor("qbh", [P, NBQ], F32))
        qbl = ec(nc.sbuf_tensor("qbl", [P, NBQ], F32))
        qr8 = ec(nc.sbuf_tensor("qr8", [P, NBQ, 8], F32))
        qr4 = ec(nc.sbuf_tensor("qr4", [P, NBQ, 4], F32))
        lnm = ec(nc.sbuf_tensor("lnm", [P, 1], F32))
        lnv = ec(nc.sbuf_tensor("lnv", [P, 1], F32))
        lnr = ec(nc.sbuf_tensor("lnr", [P, 1], F32))
        b0 = ec(nc.sbuf_tensor("b0", [P, 1], F32))
        ident = ec(nc.sbuf_tensor("ident", [P, P], BF16))
        rowidf = ec(nc.sbuf_tensor("rowidf", [P, 1], F32))
        rowtmp = ec(nc.sbuf_tensor("rowtmp", [P, 1], F32))

        def arn(lo, n):
            return arena[:, lo:lo + n]
        AW0, AW1 = 0, QFDM
        AAW0 = 2 * QFDM
        AAW1 = 3 * QFDM
        AU = 4 * QFDM
        AQ = 5 * QFDM
        AY0 = 6 * QFDM
        AY1 = 7 * QFDM
        qint = arena[:, AQ:AQ + QFDM].bitcast(I32)
        deqv = arena[:, AU:AU + QFDM // 2].bitcast(BF16)
        colidf = arena[:, AAW1:AAW1 + max(CH, P)]
        colidi = arena[:, AAW0:AAW0 + max(CH, P)].bitcast(I32)
        lnsq = arena[:, AU:AU + D]

        psA = ec(nc.psum_tensor("psA", [P, 2 * max(CH, DW)], F32))
        psB = ec(nc.psum_tensor("psB", [P, 2 * max(CH, DW)], F32))
        psC = ec(nc.psum_tensor("psC", [P, max(CH, DW)], F32))
        psD = ec(nc.psum_tensor("psD", [P, max(CH, DW)], F32))
        psE = ec(nc.psum_tensor("psE", [P, max(CH, DW)], F32))
        psT0 = psA[:, 0:P // 2].bitcast(BF16)
        psT1 = psB[:, 0:P // 2].bitcast(BF16)

        kvstage = Abuf[:, 0:2, 0:DW].rearrange("p s c -> p s c")
        pbufS = [pbuf[:, 0, 0:NKSUB * CH],
                 xbuf[:, 0:(NKSUB * CH) // 2].bitcast(BF16)]
        xn = QT[:, 0:max(1, D // TLOC), 0:min(D, TLOC)].rearrange("p a b -> p (a b)")
        xn = xn.rearrange("p (o d) -> p o d", o=1)

        sem = {}
        for nm in ["vs", "ss", "ts", "gs",
                   "dx", "dla", "dlb", "dwa", "dsa", "dsb", "dka", "dkb",
                   "dfc", "dwp", "dqa", "dqb", "dwl", "dwm", "dout",
                   "cw", "ckt", "cv", "cfc", "cwp"]:
            sem[nm] = ec(nc.semaphore(nm))
        block = ec(nc.Block())

        prog = {"sync": [], "vector": [], "scalar": [], "tensor": [], "gpsimd": []}

        def EM(eng, f, inc=None, n=1):
            if inc is None:
                prog[eng].append(f)
                return None
            k = _inc(inc, n)
            prog[eng].append(lambda e, f=f, s=sem[inc], n=n: f(e).then_inc(s, n))
            return k

        def WT(eng, key, val):
            prog[eng].append(lambda e, s=sem[key], v=val: e.wait_ge(s, v))

        V = lambda f: EM("vector", f, "vs")
        Sc = lambda f: EM("scalar", f, "ss")
        T = lambda f: EM("tensor", f, "ts")

        tr_state = {"seq": 0, "evac": {}}
        tstage_gate = [None]

        def emit_transpose(src_ap, dst_f):
            sq = tr_state["seq"]
            pv = psT0 if sq % 2 == 0 else psT1
            if sq >= 2:
                WT("tensor", "vs", tr_state["evac"][sq - 2])
            T(lambda t, src_ap=src_ap, pv=pv: t.transpose(pv, src_ap, ident[:]))
            ntr = _val("ts")
            WT("vector", "ts", ntr)
            V(lambda v, pv=pv, dst_f=dst_f: dst_f(v, pv))
            tr_state["evac"][sq] = _val("vs")
            tr_state["seq"] = sq + 1

        SA = cfg.get("stop_after", 99)

        class _Cut(Exception):
            pass

        def cut(k):
            if SA <= k:
                raise _Cut()

        try:
            EM("gpsimd", lambda g: g.iota(colidi[:, 0:max(CH, P)], pattern=[[1, max(CH, P)]],
                                          base=0, channel_multiplier=0), "gs")
            EM("gpsimd", lambda g: g.iota(qint[:, 0:1], pattern=[[0, 1]], base=0,
                                          channel_multiplier=1), "gs")
            EM("gpsimd", lambda g: g.iota(rowidH[:], pattern=[[0, 1]], base=0,
                                          channel_multiplier=1), "gs")
            n_giota = _val("gs")

            V(lambda v: v.memset(b0[:], 0.0))
            V(lambda v: v.memset(onesH[:], 1.0))
            WT("vector", "gs", n_giota)
            V(lambda v: v.tensor_copy(colidf[:, 0:max(CH, P)], colidi[:, 0:max(CH, P)]))
            V(lambda v: v.tensor_copy(rowidf[:], qint[:, 0:1]))
            V(lambda v: v.tensor_copy(rowidHf[:], rowidH[:]))
            n0 = _val("vs"); WT("vector", "vs", n0)
            V(lambda v: v.tensor_scalar(ident[:], colidf[:, 0:P], rowidf[:, 0:1], None, OP.is_equal))
            for j in range(NKSUB):
                if j > 0:
                    WT("vector", "vs", _val("vs"))
                V(lambda v, j=j: v.tensor_scalar(rowtmp[:], rowidf[:], float(P * j), None, OP.add))
                nrt = _val("vs"); WT("vector", "vs", nrt)
                V(lambda v, j=j: v.tensor_scalar(dmask[:, j, :], colidf[:, 0:CH], rowtmp[:, 0:1],
                                                 None, OP.is_ge))
            for h in range(H):
                V(lambda v, h=h: v.tensor_scalar(ocol[:, h, :], colidf[:, 0:P], float(h),
                                                 None, OP.is_equal))
            for h in range(H):
                if h > 0:
                    WT("vector", "vs", _val("vs"))
                V(lambda v, h=h: v.tensor_scalar(rowtmpH[:], rowidHf[:], float(h), None, OP.is_equal))
                nrt = _val("vs"); WT("vector", "vs", nrt)
                V(lambda v, h=h: v.tensor_scalar(selr[:, h, :], onesH[:], rowtmpH[:, 0:1],
                                                 None, OP.mult))
            n_consts = _val("vs")

            cut(0)
            wlist = [wq_d, wk_d, wv_d, wo_d]
            NW2 = max(1, DSL // P)
            PW = min(P, DSL)
            n_wcast = {}
            n_wtr = {}

            widx_r2 = [(w, r) for w in range(4) for r in range(NW2)]
            for ii, (widx, r2) in enumerate(widx_r2):
                slot = ii % 2
                wbv = arn(slot * QFDM, D)
                lsem = "dla" if slot == 0 else "dlb"
                if ii < 2:
                    WT("sync", "vs", n_consts)
                else:
                    WT("sync", "ss", n_wcast[ii - 2])
                EM("sync", lambda e, w=wlist[widx], r2=r2, wbv=wbv:
                   e.dma_start(out=wbv[0:PW, :], in_=w[r2 * P:r2 * P + PW, :]), lsem, 16)
                ndw = _val(lsem)
                WT("scalar", lsem, ndw)
                WT("scalar", "vs", n_consts)
                if ii > 0:
                    WT("scalar", "ts", n_wtr[ii - 1])
                Sc(lambda s, wbv=wbv: s.activation(
                    xn[0:PW, 0, :], wbv[0:PW, :], AF.Identity, bias=b0[0:PW, 0:1], scale=1.0))
                n_wcast[ii] = _val("ss")
                WT("tensor", "ss", n_wcast[ii])
                for g8 in range(0, NIC, 4):
                    if tstage_gate[0] is not None:
                        WT("vector", tstage_gate[0][0], tstage_gate[0][1])
                    for ic in range(g8, min(g8 + 4, NIC)):
                        def mkdst(ic=ic, g8=g8):
                            def dst(v, pv):
                                return v.tensor_copy(tstage[:, ic - g8, 0:PW], pv[0:P, 0:PW])
                            return dst
                        emit_transpose(xn[0:PW, 0, ic * P:ic * P + P], mkdst())
                    nst = _val("vs")
                    WT("sync", "vs", nst)
                    nch = min(4, NIC - g8)
                    EM("sync", lambda e, widx=widx, r2=r2, g8=g8, nch=nch:
                       e.dma_start(out=agw_in[widx * D + g8 * P: widx * D + (g8 + nch) * P,
                                              r2 * P:r2 * P + PW]
                                   .rearrange("(ic p) c -> p ic c", p=P),
                                   in_=tstage[:, 0:nch, 0:PW]), "dwa", 16)
                    tstage_gate[0] = ("dwa", _val("dwa"))
                n_wtr[ii] = _val("ts")
            WT("gpsimd", "dwa", _val("dwa"))
            EM("gpsimd", lambda g: g.collective_compute(
                "AllGather", OP.bypass, replica_groups=[list(range(NCORES))],
                ins=[agw_in.ap().opt()], outs=[agw_out.ap().opt()]), "cw")

            cut(1)
            ts_after_wprep = _val("ts")
            xsrc = [x0, x1]
            n_ln = [0] * NTT
            n_xnT = [0] * NTT
            for tt in range(NTT):
                src = xsrc[tt // (NTT // 2)]
                r0 = (tt % (NTT // 2)) * P
                if tt > 0:
                    WT("sync", "vs", n_ln[tt - 1])
                EM("sync", lambda e, src=src, r0=r0:
                   e.dma_start(out=xbuf[:, 0:D], in_=src[r0:r0 + P, :]), "dx", 16)
                ndx = _val("dx")
                cut(1.1)
                WT("vector", "dx", ndx)
                V(lambda v: v.tensor_reduce(lnm[:], xbuf[:, 0:D], AX.X, OP.add))
                cut(1.2)
                V(lambda v: v.scalar_tensor_tensor(lnsq, xbuf[:, 0:D], 1.0, xbuf[:, 0:D],
                                                   OP.mult, OP.mult, accum_out=lnv[:]))
                cut(1.3)
                n1 = _val("vs"); WT("vector", "vs", n1)
                V(lambda v: v.tensor_scalar(lnm[:], lnm[:], 1.0 / D, None, OP.mult))
                n2 = _val("vs"); WT("vector", "vs", n2)
                V(lambda v: v.tensor_tensor(lnr[:], lnm[:], lnm[:], OP.mult))
                n3 = _val("vs"); WT("vector", "vs", n3)
                V(lambda v: v.tensor_scalar(lnv[:], lnv[:], 1.0 / D, None, OP.mult))
                n4 = _val("vs"); WT("vector", "vs", n4)
                V(lambda v: v.tensor_tensor(lnv[:], lnv[:], lnr[:], OP.subtract))
                n5 = _val("vs"); WT("vector", "vs", n5)
                V(lambda v: v.tensor_scalar(lnv[:], lnv[:], LN_EPS, None, OP.add))
                n6 = _val("vs")
                WT("scalar", "vs", n6)
                Sc(lambda s: s.activation(lnr[:], lnv[:], AF.Ln, bias=b0[:, 0:1], scale=1.0))
                n7 = _val("ss"); WT("scalar", "ss", n7)
                cut(1.4)
                Sc(lambda s: s.activation(lnr[:], lnr[:], AF.Exp, bias=b0[:, 0:1], scale=-0.5))
                n8 = _val("ss")
                cut(1.5)
                WT("vector", "ss", n8)
                if tt >= 1:
                    WT("vector", "ts", n_xn_read[tt - 1])
                else:
                    WT("vector", "ts", ts_after_wprep)
                V(lambda v, tt=tt: v.tensor_scalar(xn[:, 0, :], xbuf[:, 0:D],
                                                   lnm[:, 0:1], lnr[:, 0:1], OP.subtract, OP.mult))
                n_ln[tt] = _val("vs")
                cut(1.6 + tt * 0.15)
                WT("tensor", "vs", n_ln[tt])
                for ic in range(NIC):
                    def mkdst(ic=ic, tt=tt):
                        def dst(v, pv):
                            return v.tensor_copy(xnT[:, ic, tt * P:(tt + 1) * P], pv[0:P, 0:P])
                        return dst
                    emit_transpose(xn[:, 0, ic * P:ic * P + P], mkdst())
                if tt == 0:
                    n_xn_read = {}
                n_xn_read[tt] = _val("ts")
                n_xnT[tt] = _val("vs")
                cut(1.7 + tt * 0.15)
            n_xnT_all = _val("vs")

            cut(2)
            li = [0]
            wl_use = {}

            def gload(addr_list):
                s = li[0]
                sm = "dqa" if s % 2 == 0 else "dqb"
                eng = "gpsimd" if s % 2 == 0 else "sync"
                if s >= 2:
                    WT(eng, "ts", wl_use.get(s - 2, 0))
                for (dst, src_) in addr_list:
                    EM(eng, lambda g, dst=dst, src_=src_: g.dma_start(out=dst, in_=src_), sm, 16)
                li[0] = s + 1
                return (sm, _val(sm)), s

            WT("gpsimd", "cw", 1)
            WT("sync", "cw", 1)
            stage_seq = [0]
            n_kstage = {}
            qk_evac = {}
            for widx, kind in ((0, "q"), (1, "k")):
                for h in range(H):
                    gi = widx * H + h
                    slot = li[0] % 2
                    r = (h * DK) // DSL
                    coff = (h * DK) % DSL
                    ndq, s = gload([(wbig[:, slot, :].rearrange("p (ic c) -> p ic c", c=DK),
                                     agw_out[r * 4 * D + widx * D: r * 4 * D + (widx + 1) * D,
                                             coff:coff + DK]
                                     .rearrange("(ic p) c -> p ic c", p=P))])
                    WT("tensor", ndq[0], ndq[1])
                    if gi == 0:
                        WT("tensor", "vs", n_xnT_all)
                    if gi >= 2:
                        WT("tensor", "ss", qk_evac[gi - 2])
                    pvb = psA if (h % 2 == 0) else psB
                    for ic in range(NIC):
                        for tw in range(NTW):
                            T(lambda t, pvb=pvb, slot=slot, ic=ic, tw=tw: t.matmul(
                                pvb[:, tw * TW:(tw + 1) * TW],
                                wbig[:, slot, ic * DK:(ic + 1) * DK],
                                xnT[:, ic, tw * TW:(tw + 1) * TW],
                                start=(ic == 0), stop=(ic == NIC - 1)))
                    wl_use[s] = _val("ts")
                    nmm = _val("ts")
                    WT("scalar", "ts", nmm)
                    for tw in range(NTW):
                        if kind == "q":
                            Sc(lambda s_, h=h, tw=tw, pvb=pvb: s_.activation(
                                QT[:, h, tw * TW:(tw + 1) * TW],
                                pvb[:, tw * TW:(tw + 1) * TW], AF.Identity,
                                bias=b0[:, 0:1], scale=1.0))
                        else:
                            kslot = stage_seq[0] % 2
                            ssem_ = "dsa" if kslot == 0 else "dsb"
                            if stage_seq[0] >= 2:
                                WT("scalar", n_kstage[stage_seq[0] - 2][0],
                                   n_kstage[stage_seq[0] - 2][1])
                            Sc(lambda s_, kslot=kslot, tw=tw, pvb=pvb: s_.activation(
                                kvstage[:, kslot, 0:TW], pvb[:, tw * TW:(tw + 1) * TW],
                                AF.Identity, bias=b0[:, 0:1], scale=1.0))
                            nev = _val("ss")
                            WT("sync", "ss", nev)
                            EM("sync", lambda e, h=h, tw=tw, kslot=kslot:
                               e.dma_start(out=agkt_in[h * DK:(h + 1) * DK, tw * TW:(tw + 1) * TW],
                                           in_=kvstage[0:DK, kslot, 0:TW]), ssem_, 16)
                            n_kstage[stage_seq[0]] = (ssem_, _val(ssem_))
                            stage_seq[0] += 1
                    qk_evac[gi] = _val("ss")
            TTGV = min(4, NTT)
            ICGV = max(1, min(4, D // DW))
            nseg = max(1, DW // DSL)
            segw = DW // nseg

            def pv4(tl):
                return (psA if (tl % 4) < 2 else psB)[:, (tl % 2) * DW:(tl % 2 + 1) * DW]

            v_evac = {}
            for pz in range(NTT // TTGV):
                for d2 in range(NDW):
                    gi = pz * NDW + d2
                    for icg in range(0, NIC, ICGV):
                        slot = li[0] % 2
                        lst = []
                        wv4 = wbig[:, slot, 0:ICGV * DW].rearrange("p (ic g c) -> p ic g c",
                                                                   g=nseg, c=segw)
                        for gsg in range(nseg):
                            c0 = d2 * DW + gsg * segw
                            r = c0 // DSL
                            coff = c0 % DSL
                            lst.append((wv4[:, :, gsg, :] if nseg > 1 else
                                        wbig[:, slot, :].rearrange("p (ic c) -> p ic c", c=segw),
                                        agw_out[r * 4 * D + 2 * D + icg * P:
                                                r * 4 * D + 2 * D + (icg + ICGV) * P,
                                                coff:coff + segw]
                                        .rearrange("(ic p) c -> p ic c", p=P)))
                        ndq, s = gload(lst)
                        WT("tensor", ndq[0], ndq[1])
                        if icg == 0:
                            WT("tensor", "ss", v_evac[gi - 1] if gi >= 1
                               else qk_evac[2 * H - 1])
                        for icl in range(ICGV):
                            for tl in range(TTGV):
                                tt = pz * TTGV + tl
                                T(lambda t, tl=tl, slot=slot, icl=icl, icg=icg, tt=tt: t.matmul(
                                    pv4(tl), xnT[:, icg + icl, tt * P:(tt + 1) * P],
                                    wbig[:, slot, icl * DW:(icl + 1) * DW],
                                    start=(icg == 0 and icl == 0),
                                    stop=(icg + ICGV == NIC and icl == ICGV - 1)))
                        wl_use[s] = _val("ts")
                    nmm = _val("ts")
                    WT("scalar", "ts", nmm)
                    for tl in range(TTGV):
                        tt = pz * TTGV + tl
                        kslot = stage_seq[0] % 2
                        ssem_ = "dsa" if kslot == 0 else "dsb"
                        if stage_seq[0] >= 2:
                            WT("scalar", n_kstage[stage_seq[0] - 2][0],
                               n_kstage[stage_seq[0] - 2][1])
                        Sc(lambda s_, kslot=kslot, tl=tl: s_.activation(
                            kvstage[:, kslot, 0:DW], pv4(tl), AF.Identity,
                            bias=b0[:, 0:1], scale=1.0))
                        nev = _val("ss")
                        WT("sync", "ss", nev)
                        EM("sync", lambda e, tt=tt, d2=d2, kslot=kslot:
                           e.dma_start(out=agv_in[tt * P:(tt + 1) * P, d2 * DW:(d2 + 1) * DW],
                                       in_=kvstage[:, kslot, 0:DW]), ssem_, 16)
                        n_kstage[stage_seq[0]] = (ssem_, _val(ssem_))
                        stage_seq[0] += 1
                    v_evac[gi] = _val("ss")
            WT("gpsimd", "dsa", _val("dsa"))
            WT("gpsimd", "dsb", _val("dsb"))
            EM("gpsimd", lambda g: g.collective_compute(
                "AllGather", OP.bypass, replica_groups=[list(range(NCORES))],
                ins=[agkt_in.ap().opt()], outs=[agkt_out.ap().opt()]), "ckt")
            EM("gpsimd", lambda g: g.collective_compute(
                "AllGather", OP.bypass, replica_groups=[list(range(NCORES))],
                ins=[agv_in.ap().opt()], outs=[agv_out.ap().opt()]), "cv")


            cut(3)
            n_qload = {}

            qg_cast = [0]
            qg_y1 = [0]
            qg_aw = {0: 0, 1: 0}
            qg_wb = {0: None, 1: None}
            vtrack = {"w": {}, "r": {}}

            def emit_quant(kind, idx, seq, qfd):
                nb = qfd // 64
                p = seq % 2
                wbk, awk = f"wb{p}", f"aw{p}"
                wbv = arn(p * QFDM, qfd)
                aw = arn((2 + p) * QFDM, qfd)
                uv = arn(AU, qfd)
                qiv = qint[:, 0:qfd]
                yS = [arn(AY0, qfd), arn(AY1, qfd)]
                yk = ["y0", "y1"]
                scbS = [qscb, qbh, qscb2]
                rS = [qinvb, qbl]
                src_t = wfc_d if kind == "fc" else wp_d
                lsem = "dla" if p == 0 else "dlb"
                VW, VR = vtrack["w"], vtrack["r"]

                def VO(f, reads=(), writes=(), ss=()):
                    need = 0
                    for k in reads:
                        need = max(need, VW.get(k, 0))
                    for k in writes:
                        need = max(need, VW.get(k, 0), VR.get(k, 0))
                    if need:
                        WT("vector", "vs", need)
                    for s_ in ss:
                        if s_:
                            WT("vector", "ss", s_)
                    kv = V(f)
                    for k in reads:
                        VR[k] = max(VR.get(k, 0), kv)
                    for k in writes:
                        VW[k] = kv
                    return kv

                if qg_wb[p] is not None:
                    WT("sync", "vs", max(qg_wb[p], VR.get(wbk, 0)))
                else:
                    WT("sync", "vs", n_xnT_all)
                    WT("sync", "ss", n_wcast[len(widx_r2) - 1])
                EM("sync", lambda e, src_t=src_t, idx=idx, wbv=wbv, qfd=qfd:
                   e.dma_start(out=wbv, in_=src_t[idx * P:(idx + 1) * P, 0:qfd]), lsem, 16)
                ndw = _val(lsem)

                R15 = float(np.float32(1.0) / np.float32(15.0))
                nbs = (P, nb, 64)
                r3 = lambda t: t.rearrange("p (s k) -> p s k", k=64)

                WT("scalar", lsem, ndw)
                WT("scalar", "vs", max(qg_aw[p], VR.get(awk, 0)) or n_consts)
                Sc(lambda s, wbv=wbv, aw=aw: s.activation(aw, wbv, AF.Abs,
                                                          bias=b0[:, 0:1], scale=1.0))
                nabs = _val("ss")

                WT("vector", lsem, ndw)
                VO(lambda v, wbv=wbv, nb=nb: v.tensor_reduce(
                    amax[:, 0:nb].rearrange("p (s o) -> p s o", o=1),
                    r3(wbv), AX.X, OP.max, apply_absolute_value=True),
                   reads=(wbk,), writes=("amax",))
                VO(lambda v, nb=nb: v.memset(qbest[:, 0:nb], 3.0e38), writes=("qbest",))
                VO(lambda v, nb=nb: v.tensor_scalar(qt1[:, 0:nb], amax[:, 0:nb],
                                                    1e-8, None, OP.max),
                   reads=("amax",), writes=("qt1",))
                VO(lambda v, nb=nb: v.memset(qbsc[:, 0:nb], 0.0), writes=("qbsc",))
                VO(lambda v, nb=nb: v.tensor_scalar(qbase[:, 0:nb], qt1[:, 0:nb],
                                                    R15, None, OP.mult),
                   reads=("qt1",), writes=("qbase",))

                cast_v = {}
                sq_v = {}

                def e_scb(j):
                    js = j % 3
                    VO(lambda v, nb=nb, js=js, s_j=SHRINKS[j]: v.tensor_scalar(
                        scbS[js][:, 0:nb], qbase[:, 0:nb], s_j, None, OP.mult),
                       reads=("qbase",), writes=(f"scb{js}",))

                def e_recip(j):
                    jp = j % 2
                    js = j % 3
                    VO(lambda v, nb=nb, jp=jp, js=js: v.reciprocal(rS[jp][:, 0:nb],
                                                                   scbS[js][:, 0:nb]),
                       reads=(f"scb{js}",), writes=(f"r{jp}",))

                u_v = {}

                def e_u(j):
                    jp = j % 2
                    u_v[j] = VO(lambda v, aw=aw, uv=uv, nb=nb, jp=jp: v.tensor_tensor(
                        r3(uv), r3(aw), rS[jp][:, 0:nb].broadcast_to(nbs), OP.mult),
                       reads=(awk, f"r{jp}"), writes=("u",),
                       ss=(qg_cast[0], nabs if j == 0 else 0))

                def e_cast(j):
                    WT("scalar", "vs", max(u_v[j], qg_y1[0]))
                    Sc(lambda s, uv=uv, qiv=qiv: s.activation(
                        qiv, uv, AF.Identity, bias=b0[:, 0:1], scale=1.0))
                    cast_v[j] = _val("ss")
                    qg_cast[0] = cast_v[j]

                def e_y1(j):
                    jp = j % 2
                    js = j % 3
                    VO(lambda v, qiv=qiv, nb=nb, jp=jp, js=js: v.scalar_tensor_tensor(
                        r3(yS[jp]), r3(qiv), QMAX,
                        scbS[js][:, 0:nb].broadcast_to(nbs), OP.min, OP.mult),
                       reads=("qint", f"scb{js}"), writes=(yk[jp],), ss=(cast_v[j],))
                    qg_y1[0] = _val("vs")

                def e_y2(j):
                    jp = j % 2
                    ny2 = VO(lambda v, aw=aw, jp=jp: v.tensor_tensor(
                        yS[jp], yS[jp], aw, OP.subtract),
                       reads=(yk[jp], awk), writes=(yk[jp],))
                    qg_aw[p] = ny2
                    WT("scalar", "vs", ny2)
                    Sc(lambda s, jp=jp: s.activation(yS[jp], yS[jp], AF.Square,
                                                     bias=b0[:, 0:1], scale=1.0))
                    sq_v[j] = _val("ss")

                def e_red(j, step):
                    jp = j % 2
                    if step == 0:
                        VO(lambda v, nb=nb, jp=jp: v.tensor_reduce(
                            qr8[:, 0:nb, :].rearrange("p s (k o) -> p s k o", o=1),
                            yS[jp].rearrange("p (s i k) -> p s k i", i=8, k=8), AX.X, OP.add),
                           reads=(yk[jp],), writes=("qr8",), ss=(sq_v[j],))
                    elif step == 1:
                        VO(lambda v, nb=nb: v.tensor_tensor(qr4[:, 0:nb, :], qr8[:, 0:nb, 0::2],
                                                            qr8[:, 0:nb, 1::2], OP.add),
                           reads=("qr8",), writes=("qr4",))
                    elif step == 2:
                        VO(lambda v, nb=nb: v.tensor_tensor(qr8[:, 0:nb, 0:2], qr4[:, 0:nb, 0::2],
                                                            qr4[:, 0:nb, 1::2], OP.add),
                           reads=("qr4",), writes=("qr8",))
                    elif step == 3:
                        VO(lambda v, nb=nb: v.tensor_tensor(
                            qsse[:, 0:nb].rearrange("p (s o) -> p s o", o=1),
                            qr8[:, 0:nb, 0:1], qr8[:, 0:nb, 1:2], OP.add),
                           reads=("qr8",), writes=("qsse",))
                    elif step == 4:
                        VO(lambda v, nb=nb: v.tensor_tensor(qmaskb[:, 0:nb], qsse[:, 0:nb],
                                                            qbest[:, 0:nb], OP.is_lt),
                           reads=("qsse", "qbest"), writes=("qmask",))
                    elif step == 5:
                        js = j % 3
                        VO(lambda v, nb=nb, js=js: v.tensor_tensor(qt1[:, 0:nb], scbS[js][:, 0:nb],
                                                                   qbsc[:, 0:nb], OP.subtract),
                           reads=(f"scb{js}", "qbsc"), writes=("qt1",))
                    elif step == 6:
                        VO(lambda v, nb=nb: v.tensor_tensor(qbest[:, 0:nb], qsse[:, 0:nb],
                                                            qbest[:, 0:nb], OP.min),
                           reads=("qsse", "qbest"), writes=("qbest",))
                    elif step == 7:
                        VO(lambda v, nb=nb: v.tensor_tensor(qt1[:, 0:nb], qt1[:, 0:nb],
                                                            qmaskb[:, 0:nb], OP.mult),
                           reads=("qt1", "qmask"), writes=("qt1",))
                    else:
                        VO(lambda v, nb=nb: v.tensor_tensor(qbsc[:, 0:nb], qbsc[:, 0:nb],
                                                            qt1[:, 0:nb], OP.add),
                           reads=("qbsc", "qt1"), writes=("qbsc",))

                e_scb(0)
                e_recip(0)
                for j in range(16):
                    e_u(j)
                    if j >= 2:
                        e_red(j - 2, 0)
                    if j >= 1:
                        e_y1(j - 1)
                    if j >= 2:
                        e_red(j - 2, 1)
                    e_cast(j)
                    if j >= 1:
                        e_y2(j - 1)
                    if j >= 2:
                        e_red(j - 2, 2)
                        e_red(j - 2, 3)
                        e_red(j - 2, 4)
                        e_red(j - 2, 5)
                    if j < 15:
                        e_scb(j + 1)
                    if j >= 2:
                        e_red(j - 2, 7)
                        e_red(j - 2, 6)
                    if j < 15:
                        e_recip(j + 1)
                    if j >= 2:
                        e_red(j - 2, 8)
                e_red(14, 0)
                e_y1(15)
                e_red(14, 1)
                e_y2(15)
                for st in (2, 3, 4, 5, 7, 6, 8):
                    e_red(14, st)
                for st in (0, 1, 2, 3, 4, 5, 7, 6, 8):
                    e_red(15, st)

                rbb = qrb[:, 0:nb].broadcast_to(nbs)
                bscb = qbsc[:, 0:nb].broadcast_to(nbs)
                VO(lambda v, nb=nb: v.reciprocal(qrb[:, 0:nb], qbsc[:, 0:nb]),
                   reads=("qbsc",), writes=("qrb",))
                ntf = VO(lambda v, wbv=wbv, uv=uv: v.tensor_tensor(r3(uv), r3(wbv), rbb, OP.mult),
                         reads=(wbk, "qrb"), writes=("u",), ss=(qg_cast[0],))
                qg_wb[p] = ntf
                ncl = VO(lambda v, uv=uv: v.tensor_scalar(uv, uv, CLIP_C, -CLIP_C, OP.min, OP.max),
                         reads=("u",), writes=("u",))
                WT("scalar", "vs", max(ncl, qg_y1[0]))
                Sc(lambda s, uv=uv, qiv=qiv: s.activation(
                    qiv, uv, AF.Identity, bias=b0[:, 0:1], scale=1.0))
                nqf = _val("ss")
                qg_cast[0] = nqf
                ndeq = VO(lambda v, nb=nb, qfd=qfd: v.tensor_tensor(
                    r3(deqv[:, 0:qfd]), r3(qint[:, 0:qfd]), bscb, OP.mult),
                    reads=("qint", "qbsc"), writes=("u",), ss=(nqf,))
                WT("tensor", "vs", ndeq)
                nchv = qfd // P
                for g8 in range(0, nchv, 4):
                    if tstage_gate[0] is not None:
                        WT("vector", tstage_gate[0][0], tstage_gate[0][1])
                    for ic in range(g8, min(g8 + 4, nchv)):
                        def mkdst(ic=ic, g8=g8):
                            def dst(v, pv):
                                return v.tensor_copy(tstage[:, ic - g8, :], pv[0:P, 0:P])
                            return dst
                        emit_transpose(deqv[:, ic * P:(ic + 1) * P], mkdst())
                    nstg = _val("vs")
                    WT("sync", "vs", nstg)
                    nch = min(4, nchv - g8)
                    dsem = "dfc" if kind == "fc" else "dwp"
                    dstT = agfc_in if kind == "fc" else agwp_in
                    EM("sync", lambda e, idx=idx, g8=g8, nch=nch, dstT=dstT:
                       e.dma_start(out=dstT[g8 * P:(g8 + nch) * P, idx * P:(idx + 1) * P]
                                   .rearrange("(ic p) c -> p ic c", p=P),
                                   in_=tstage[:, 0:nch, :]), dsem, 16)
                    tstage_gate[0] = (dsem, _val(dsem))
                VW["u"] = max(VW["u"], _val("vs"))
                n_qdone[seq] = _val("vs")

            n_qdone = {}
            seq = 0
            for idx in range(NQT_FC):
                emit_quant("fc", idx, seq, QFD_FC)
                seq += 1
            WT("gpsimd", "dfc", _val("dfc"))
            EM("gpsimd", lambda g: g.collective_compute(
                "AllGather", OP.bypass, replica_groups=[list(range(NCORES))],
                ins=[agfc_in.ap().opt()], outs=[agfc_out.ap().opt()]), "cfc")
            for idx in range(NQT_WP):
                emit_quant("wp", idx, seq, QFD_WP)
                seq += 1
            WT("gpsimd", "dwp", _val("dwp"))
            EM("gpsimd", lambda g: g.collective_compute(
                "AllGather", OP.bypass, replica_groups=[list(range(NCORES))],
                ins=[agwp_in.ap().opt()], outs=[agwp_out.ap().opt()]), "cwp")
            n_quant_v = _val("vs")
            if cfg.get("debug_deq"):
                WT("sync", "dfc", _val("dfc"))
                WT("sync", "dwp", _val("dwp"))
                EM("sync", lambda e: e.dma_start(out=dbg_fc, in_=agfc_in[:, :]), "dout", 16)
                EM("sync", lambda e: e.dma_start(out=dbg_wp, in_=agwp_in[:, :]), "dout", 16)

            cut(4)
            WT("sync", "ckt", 1)
            WT("sync", "cv", 1)
            WT("tensor", "vs", n_quant_v)
            arm_prog = {e: [[] for _ in range(NCORES)] for e in ("sync", "vector", "scalar", "tensor")}

            def build_arm(c, ap_loc):
                def AEM(eng, f, inc, n=1):
                    k = _inc(inc, n)
                    ap_loc[eng].append(lambda e, f=f, s=sem[inc], n=n: f(e).then_inc(s, n))
                    return k

                def AWT(eng, key, val):
                    ap_loc[eng].append(lambda e, s=sem[key], v=val: e.wait_ge(s, v))

                kv_seq = 0
                kv_use = {}
                pair_seq = 0
                pair_exp = {}
                for qc in range(2):
                    qpos = c if qc == 0 else (NCORES - 1 - c)
                    nblk = qpos + 1
                    cs_total = H * nblk * NKSUB
                    cs_i = 0
                    for h in range(H):
                        for kb in range(nblk):
                            slot = kv_seq % 2
                            if qc == 0:
                                krow = kb * D + h * DK
                                kcol = 0
                                vrow = kb * TLOC
                            else:
                                krow = (NCORES - 1 - kb) * D + h * DK
                                kcol = CH
                                vrow = (NCORES - 1 - kb) * TLOC + CH
                            if kv_seq >= 2:
                                AWT("sync", "ts", kv_use[kv_seq - 2])
                            ksem = "dka" if slot == 0 else "dkb"
                            AEM("sync", lambda e, krow=krow, kcol=kcol, slot=slot:
                                e.dma_start(out=kbuf[:, slot, :],
                                            in_=agkt_out[krow:krow + DK, kcol:kcol + CH]),
                                ksem, 16)
                            AEM("sync", lambda e, vrow=vrow, h=h, slot=slot:
                                e.dma_start(out=vbuf[:, slot, :].rearrange("p (j d) -> p j d", d=DK),
                                            in_=agv_out[vrow:vrow + CH, h * DK:(h + 1) * DK]
                                            .rearrange("(j p) d -> p j d", p=P)),
                                ksem, 16)
                            ndkv = _val(ksem)
                            diag = (kb == nblk - 1)
                            AWT("tensor", ksem, ndkv)
                            for jp in range(0, NKSUB, 2):
                                wpair = min(2, NKSUB - jp)
                                pps = psA if (pair_seq % 2 == 0) else psB
                                if pair_seq >= 2:
                                    AWT("tensor", "ss", pair_exp[pair_seq - 2])
                                for j in range(jp, jp + wpair):
                                    AEM("tensor", lambda t, pps=pps, j=j, jp=jp, slot=slot, h=h, qc=qc:
                                        t.matmul(pps[:, (j - jp) * CH:(j - jp + 1) * CH],
                                                 kbuf[:, slot, j * P:(j + 1) * P],
                                                 QT[:, h, qc * CH:(qc + 1) * CH],
                                                 start=True, stop=True), "ts")
                                nS = _val("ts")
                                AWT("scalar", "ts", nS)
                                if kv_seq >= 2 and jp == 0:
                                    AWT("scalar", "ts", kv_use[kv_seq - 2])
                                AEM("scalar", lambda s, pps=pps, jp=jp, wpair=wpair, slot=slot:
                                    s.activation(pbufS[slot][:, jp * CH:(jp + wpair) * CH],
                                                 pps[:, 0:wpair * CH], AF.Exp,
                                                 bias=b0[:, 0:1], scale=SQRT_DK_INV), "ss")
                                pair_exp[pair_seq] = _val("ss")
                                pair_seq += 1
                            nexp = _val("ss")
                            if diag:
                                AWT("vector", "ss", nexp)
                                for j in range(NKSUB):
                                    AEM("vector", lambda v, j=j, slot=slot:
                                        v.tensor_tensor(pbufS[slot][:, j * CH:(j + 1) * CH],
                                                        pbufS[slot][:, j * CH:(j + 1) * CH],
                                                        dmask[:, j, :], OP.mult), "vs")
                                npm = _val("vs")
                                AWT("tensor", "vs", npm)
                            else:
                                AWT("tensor", "ss", nexp)
                            for j in range(NKSUB):
                                AEM("tensor", lambda t, j=j, h=h, slot=slot,
                                    st_=(cs_i == 0), sp_=(cs_i == cs_total - 1):
                                    t.matmul(psD[:, 0:CH], ocol[:, h, :],
                                             pbufS[slot][:, j * CH:(j + 1) * CH],
                                             start=st_, stop=sp_), "ts")
                                cs_i += 1
                                AEM("tensor", lambda t, j=j, slot=slot,
                                    st_=(kb == 0 and j == 0),
                                    sp_=(kb == nblk - 1 and j == NKSUB - 1):
                                    t.matmul(psC[:, 0:CH], vbuf[:, slot, j * P:(j + 1) * P],
                                             pbufS[slot][:, j * CH:(j + 1) * CH],
                                             start=st_, stop=sp_), "ts")
                            kv_use[kv_seq] = _val("ts")
                            kv_seq += 1
                        nAV = _val("ts")
                        AWT("scalar", "ts", nAV)
                        AEM("scalar", lambda s, h=h, qc=qc: s.activation(
                            Abuf[:, h, qc * CH:(qc + 1) * CH], psC[:, 0:CH], AF.Identity,
                            bias=b0[:, 0:1], scale=1.0), "ss")
                        nevA = _val("ss")
                        AWT("tensor", "ss", nevA)
                    ncs = _val("ts")
                    AWT("vector", "ts", ncs)
                    AEM("vector", lambda v: v.tensor_copy(bcb[:, 0:CH], psD[:, 0:CH]), "vs")
                    nevc = _val("vs")
                    AWT("tensor", "vs", nevc)
                    AWT("vector", "vs", nevc)
                    AEM("vector", lambda v: v.reciprocal(bcb[0:H, 0:CH], bcb[0:H, 0:CH]), "vs")
                    nrc = _val("vs")
                    AWT("vector", "vs", nrc)
                    AEM("vector", lambda v: v.tensor_copy(rcb[:, 0:CH], bcb[0:H, 0:CH]), "vs")
                    nrcb = _val("vs")
                    last_ss = _val("ss")
                    for h in range(H):
                        AWT("tensor", "vs", nrcb)
                        AEM("tensor", lambda t, h=h: t.matmul(
                            psE[:, 0:CH], selr[:, h, :], rcb[:, 0:CH],
                            start=True, stop=True), "ts")
                        nbc = _val("ts")
                        AWT("vector", "ts", nbc)
                        AEM("vector", lambda v: v.tensor_copy(bcb[:, 0:CH], psE[:, 0:CH]), "vs")
                        nbe = _val("vs")
                        AWT("tensor", "vs", nbe)
                        AWT("vector", "vs", nbe)
                        AWT("vector", "ss", last_ss)
                        AEM("vector", lambda v, h=h, qc=qc: v.tensor_tensor(
                            Abuf[:, h, qc * CH:(qc + 1) * CH],
                            Abuf[:, h, qc * CH:(qc + 1) * CH], bcb[:, 0:CH], OP.mult), "vs")
                        nno = _val("vs")
                        AWT("vector", "vs", nno)

            snap0 = dict(ctr)
            deltas = None
            for c in range(NCORES):
                ctr.clear(); ctr.update(snap0)
                build_arm(c, {e: arm_prog[e][c] for e in arm_prog})
                d = {k: ctr[k] - snap0.get(k, 0) for k in ctr}
                d = {k: v for k, v in d.items() if v}
                if deltas is None:
                    deltas = d
                else:
                    assert d == deltas, f"arm {c}: {d} != {deltas}"

            for ename in ("sync", "vector", "scalar", "tensor"):
                def mk(ename=ename):
                    def run(e):
                        pid = e.partition_id()
                        for armc in e.Switch(pid, NCORES):
                            for f in arm_prog[ename][armc]:
                                f(e)
                    return run
                prog[ename].append(mk())
            n_attn_v = _val("vs")
            n_attn_t = _val("ts")
            n_attn_s = _val("ss")

            cut(5)
            TTGO = max(1, min(4, NTT // 2))
            o_acc = {}
            for pz in range(NTT // TTGO):
                src_x = xsrc[(pz * TTGO) // (NTT // 2)]
                r0x = ((pz * TTGO) % (NTT // 2)) * P
                for d2 in range(NDW):
                    gi = pz * NDW + d2
                    if gi == 0:
                        WT("sync", "vs", n_quant_v)
                        WT("sync", "ts", n_attn_t)
                    else:
                        WT("sync", "vs", o_acc[gi - 1])
                    EM("sync", lambda e, src_x=src_x, r0x=r0x, d2=d2:
                       e.dma_start(out=xbuf[:, 0:TTGO * DW].rearrange("p (tl c) -> p tl c", c=DW),
                                   in_=src_x[r0x:r0x + TTGO * P, d2 * DW:(d2 + 1) * DW]
                                   .rearrange("(tl p) c -> p tl c", p=P)), "dx", 16)
                    ndx = _val("dx")
                    for kcg in range(0, H, ICGV):
                        slot = li[0] % 2
                        lst = []
                        wv4 = wbig[:, slot, 0:ICGV * DW].rearrange("p (kc g c) -> p kc g c",
                                                                   g=nseg, c=segw)
                        for gsg in range(nseg):
                            c0 = d2 * DW + gsg * segw
                            r = c0 // DSL
                            coff = c0 % DSL
                            lst.append((wv4[:, :, gsg, :] if nseg > 1 else
                                        wbig[:, slot, :].rearrange("p (kc c) -> p kc c", c=segw),
                                        agw_out[r * 4 * D + 3 * D + kcg * P:
                                                r * 4 * D + 3 * D + (kcg + ICGV) * P,
                                                coff:coff + segw]
                                        .rearrange("(kc p) c -> p kc c", p=P)))
                        ndq, s = gload(lst)
                        WT("tensor", ndq[0], ndq[1])
                        if gi == 0 and kcg == 0:
                            WT("tensor", "vs", n_attn_v)
                            WT("tensor", "vs", n_quant_v)
                        if kcg == 0 and gi >= 1:
                            WT("tensor", "vs", o_acc[gi - 1])
                        for kcl in range(ICGV):
                            for tl in range(TTGO):
                                tt = pz * TTGO + tl
                                T(lambda t, tl=tl, slot=slot, kcl=kcl, kcg=kcg, tt=tt: t.matmul(
                                    pv4(tl), Abuf[:, kcg + kcl, tt * P:(tt + 1) * P],
                                    wbig[:, slot, kcl * DW:(kcl + 1) * DW],
                                    start=(kcg == 0 and kcl == 0),
                                    stop=(kcg + ICGV == H and kcl == ICGV - 1)))
                        wl_use[s] = _val("ts")
                    nmm = _val("ts")
                    WT("vector", "ts", nmm)
                    WT("vector", "dx", ndx)
                    if gi == 0:
                        WT("vector", "vs", n_quant_v)
                    for tl in range(TTGO):
                        tt = pz * TTGO + tl
                        V(lambda v, tt=tt, d2=d2, tl=tl: v.tensor_tensor(
                            arena[:, tt * D + d2 * DW: tt * D + (d2 + 1) * DW],
                            xbuf[:, tl * DW:(tl + 1) * DW], pv4(tl), OP.add))
                    o_acc[gi] = _val("vs")
            n_oproj_t = _val("ts")

            cut(6)
            WT("gpsimd", "cfc", 1)
            WT("sync", "cfc", 1)
            pacc = {}
            wp_mm = {}
            fc_prev = [0]
            fc_sq = {}
            for sl in range(NSLAB):
                if sl >= 1:
                    WT("vector", "ts", wp_mm[(sl - 1) * NDW + NDW - 1])
                for hcl in range(NHC_SLAB):
                    hc = sl * NHC_SLAB + hcl
                    slot = li[0] % 2
                    r = (hc * P) // HIDC
                    coff = (hc * P) % HIDC
                    ndq, s = gload([(wbig[:, slot, :].rearrange("p (ic c) -> p ic c", c=P),
                                     agfc_out[r * D:(r + 1) * D, coff:coff + P]
                                     .rearrange("(ic p) c -> p ic c", p=P))])
                    WT("tensor", ndq[0], ndq[1])
                    if sl == 0 and hcl == 0:
                        WT("tensor", "vs", n_attn_v)
                    else:
                        WT("tensor", "ss", fc_prev[0])
                    for ic in range(NIC):
                        for tw in range(NTW):
                            T(lambda t, slot=slot, ic=ic, tw=tw: t.matmul(
                                (psC if tw == 0 else psD)[:, 0:TW],
                                wbig[:, slot, ic * P:(ic + 1) * P],
                                xnT[:, ic, tw * TW:(tw + 1) * TW],
                                start=(ic == 0), stop=(ic == NIC - 1)))
                    wl_use[s] = _val("ts")
                    nmm = _val("ts")
                    WT("scalar", "ts", nmm)
                    for tw in range(NTW):
                        rlv = pbuf[:, 0, 0:4 * TW].bitcast(F32)[:, tw * TW:(tw + 1) * TW]
                        if fc_sq.get(tw):
                            WT("scalar", "vs", fc_sq[tw])
                        Sc(lambda s_, tw=tw, rlv=rlv: s_.activation(
                            rlv, (psC if tw == 0 else psD)[:, 0:TW],
                            AF.Relu, bias=b0[:, 0:1], scale=1.0))
                        nrl = _val("ss")
                        WT("vector", "ss", nrl)
                        V(lambda v, hcl=hcl, tw=tw, rlv=rlv: v.tensor_tensor(
                            QT[:, hcl, tw * TW:(tw + 1) * TW], rlv, rlv, OP.mult))
                        fc_sq[tw] = _val("vs")
                    fc_prev[0] = _val("ss")
                n_gslab = _val("vs")
                if sl == 0:
                    WT("sync", "cwp", 1)
                    WT("sync", "ts", n_oproj_t)
                for d2 in range(NDW):
                    dp = d2 % 2
                    lin = sl * NDW + d2
                    wsem = "dwl" if dp == 0 else "dwm"
                    if lin >= 2:
                        WT("sync", "ts", wp_mm[lin - 2])
                    EM("sync", lambda e, sl=sl, d2=d2, dp=dp:
                       e.dma_start(out=Abuf[:, 0:NHC_SLAB, dp * DW:(dp + 1) * DW],
                                   in_=agwp_out[sl * NHC_SLAB * P:(sl + 1) * NHC_SLAB * P,
                                                d2 * DW:(d2 + 1) * DW]
                                   .rearrange("(hcl p) c -> p hcl c", p=P)), wsem, 16)
                    ndwl = _val(wsem)
                    WT("tensor", wsem, ndwl)
                    if d2 == 0:
                        WT("tensor", "vs", n_gslab)
                    for tt in range(NTT):
                        pv = (psA if (tt % 2 == 0) else psB)[:, dp * DW: dp * DW + DW]
                        bk = (tt % 2, dp)
                        if pacc.get(bk):
                            WT("tensor", "vs", pacc[bk])
                        for hcl in range(NHC_SLAB):
                            T(lambda t, pv=pv, hcl=hcl, tt=tt, dp=dp: t.matmul(
                                pv, QT[:, hcl, tt * P:(tt + 1) * P],
                                Abuf[:, hcl, dp * DW:(dp + 1) * DW],
                                start=(hcl == 0), stop=(hcl == NHC_SLAB - 1)))
                        nmm = _val("ts")
                        WT("vector", "ts", nmm)
                        V(lambda v, tt=tt, d2=d2, pv=pv: v.tensor_tensor(
                            arena[:, tt * D + d2 * DW: tt * D + (d2 + 1) * DW],
                            arena[:, tt * D + d2 * DW: tt * D + (d2 + 1) * DW], pv, OP.add))
                        pacc[bk] = _val("vs")
                    wp_mm[lin] = _val("ts")

            n_ffn_v = _val("vs")

            cut(7)
            WT("sync", "vs", n_ffn_v)
            for tt in range(NTT):
                dst = out0 if tt < NTT // 2 else out1
                r0 = (tt % (NTT // 2)) * P
                EM("sync", lambda e, dst=dst, r0=r0, tt=tt:
                   e.dma_start(out=dst[r0:r0 + P, :], in_=arena[:, tt * D:(tt + 1) * D]),
                   "dout", 16)
            WT("sync", "dout", _val("dout"))


        except _Cut:
            WT("sync", "vs", _val("vs"))
            WT("sync", "ss", _val("ss"))
            WT("sync", "ts", _val("ts"))
            WT("sync", "gs", _val("gs"))
            for tt in range(NTT):
                dst = out0 if tt < NTT // 2 else out1
                r0 = (tt % (NTT // 2)) * P
                EM("sync", lambda e, dst=dst, r0=r0, tt=tt:
                   e.dma_start(out=dst[r0:r0 + P, :], in_=arena[:, 0:D]), "dout", 16)
            WT("sync", "dout", _val("dout"))

        @block.sync
        def _(e):
            for f in prog["sync"]:
                f(e)

        @block.vector
        def _(e):
            for f in prog["vector"]:
                f(e)

        @block.scalar
        def _(e):
            for f in prog["scalar"]:
                f(e)

        @block.tensor
        def _(e):
            for f in prog["tensor"]:
                f(e)

        @block.gpsimd
        def _(e):
            for f in prog["gpsimd"]:
                f(e)

    return nc


_CACHE = {}


def kernel(L0_attn_in_features, L0_attn_q_proj, L0_attn_k_proj, L0_attn_v_proj,
           L0_attn_o_proj, L0_ffn_w_fc, L0_ffn_w_proj):
    cfg = REAL_CFG
    D = cfg["D"]; CH = cfg["CH"]; HID = cfg["HID"]
    HIDC = HID // NCORES
    DSL = D // NCORES
    x = np.ascontiguousarray(L0_attn_in_features)
    if "real" not in _CACHE:
        _CACHE["real"] = build_program(cfg)
    nc = _CACHE["real"]
    in_maps = []
    for c in range(NCORES):
        cc = NCORES - 1 - c
        in_maps.append({
            "x0": np.ascontiguousarray(x[0, c * CH:(c + 1) * CH, :]),
            "x1": np.ascontiguousarray(x[1, cc * CH:(cc + 1) * CH, :]),
            "wq": np.ascontiguousarray(L0_attn_q_proj[c * DSL:(c + 1) * DSL, :]),
            "wk": np.ascontiguousarray(L0_attn_k_proj[c * DSL:(c + 1) * DSL, :]),
            "wv": np.ascontiguousarray(L0_attn_v_proj[c * DSL:(c + 1) * DSL, :]),
            "wo": np.ascontiguousarray(L0_attn_o_proj[c * DSL:(c + 1) * DSL, :]),
            "wfc": np.ascontiguousarray(L0_ffn_w_fc[c * HIDC:(c + 1) * HIDC, :]),
            "wp": np.ascontiguousarray(L0_ffn_w_proj[:, c * HIDC:(c + 1) * HIDC]),
        })
    res = run_bass_kernel_spmd(nc, in_maps, core_ids=list(range(NCORES)))
    S = NCORES * CH
    out = np.empty((2, S, D), np.float32)
    for c in range(NCORES):
        cc = NCORES - 1 - c
        out[0, c * CH:(c + 1) * CH, :] = res.results[c]["out0"]
        out[1, cc * CH:(cc + 1) * CH, :] = res.results[c]["out1"]
    return out


# version 22 (final)
# speedup vs baseline: 1.0130x; 1.0130x over previous
import math
import contextlib
import numpy as np

import concourse.bass as bass
import concourse.mybir as mybir
from concourse.bass_utils import run_bass_kernel_spmd

F32 = mybir.dt.float32
BF16 = mybir.dt.bfloat16
I32 = mybir.dt.int32
AF = mybir.ActivationFunctionType
OP = mybir.AluOpType
AX = mybir.AxisListType

NCORES = 8
P = 128

REAL_CFG = dict(D=2048, H=16, CH=512, HID=8192)
SHRINKS = [float(np.float32(s)) for s in np.linspace(0.4, 1.0, 16)]
CLIP_C = 15.25
QMAX = 15.0
SQRT_DK_INV = float(np.float32(1.0 / math.sqrt(128.0)))
LN_EPS = 1e-5


def build_program(cfg):
    D = cfg["D"]; H = cfg["H"]; CH = cfg["CH"]; HID = cfg["HID"]
    DK = 128
    assert D == H * DK and CH % P == 0
    TLOC = 2 * CH
    NTT = TLOC // P
    DSL = D // NCORES
    HIDC = HID // NCORES
    NIC = D // P
    NKSUB = CH // P
    TW = min(512, TLOC)
    DW = min(512, D)
    NTW = TLOC // TW
    NDW = D // DW
    assert DSL % P == 0 or DSL < P
    QFD_FC = D
    QFD_WP = HIDC
    QFDM = max(QFD_FC, QFD_WP)
    NQT_FC = HIDC // P
    NQT_WP = D // P
    NBQ = QFDM // 64
    FFN_SLAB = min(2048, HID)
    NSLAB = HID // FFN_SLAB
    NHC_SLAB = FFN_SLAB // P
    assert H >= NHC_SLAB or True
    NCH2 = max(H, NHC_SLAB)

    nc = bass.Bass("TRN2", target_bir_lowering=False, debug=False)

    x0 = nc.dram_tensor("x0", [CH, D], F32, kind="ExternalInput").ap()
    x1 = nc.dram_tensor("x1", [CH, D], F32, kind="ExternalInput").ap()
    wq_d = nc.dram_tensor("wq", [DSL, D], F32, kind="ExternalInput").ap()
    wk_d = nc.dram_tensor("wk", [DSL, D], F32, kind="ExternalInput").ap()
    wv_d = nc.dram_tensor("wv", [DSL, D], F32, kind="ExternalInput").ap()
    wo_d = nc.dram_tensor("wo", [DSL, D], F32, kind="ExternalInput").ap()
    wfc_d = nc.dram_tensor("wfc", [HIDC, D], F32, kind="ExternalInput").ap()
    wp_d = nc.dram_tensor("wp", [D, HIDC], F32, kind="ExternalInput").ap()
    out0 = nc.dram_tensor("out0", [CH, D], F32, kind="ExternalOutput").ap()
    out1 = nc.dram_tensor("out1", [CH, D], F32, kind="ExternalOutput").ap()

    agw_in = nc.dram_tensor("agw_in", [4 * D, DSL], BF16)
    agw_out = nc.dram_tensor("agw_out", [NCORES * 4 * D, DSL], BF16)
    agkt_in = nc.dram_tensor("agkt_in", [D, TLOC], BF16)
    agkt_out = nc.dram_tensor("agkt_out", [NCORES * D, TLOC], BF16)
    agv_in = nc.dram_tensor("agv_in", [TLOC, D], BF16)
    agv_out = nc.dram_tensor("agv_out", [NCORES * TLOC, D], BF16)
    agfc_in = nc.dram_tensor("agfc_in", [D, HIDC], BF16)
    agfc_out = nc.dram_tensor("agfc_out", [NCORES * D, HIDC], BF16)
    agwp_in = nc.dram_tensor("agwp_in", [HIDC, D], BF16)
    agwp_out = nc.dram_tensor("agwp_out", [NCORES * HIDC, D], BF16)
    if cfg.get("debug_deq"):
        dbg_fc = nc.dram_tensor("dbg_fc", [D, HIDC], BF16, kind="ExternalOutput").ap()
        dbg_wp = nc.dram_tensor("dbg_wp", [HIDC, D], BF16, kind="ExternalOutput").ap()

    ctr = {}

    def _inc(key, n):
        ctr[key] = ctr.get(key, 0) + n
        return ctr[key]

    def _val(key):
        return ctr.get(key, 0)

    with contextlib.ExitStack() as stk:
        ec = stk.enter_context

        ARN = max(8 * QFDM, NTT * D)
        arena = ec(nc.sbuf_tensor("arena", [P, ARN], F32))
        xnT = ec(nc.sbuf_tensor("xnT", [P, NIC, TLOC], BF16))
        QT = ec(nc.sbuf_tensor("QT", [P, NCH2, TLOC], BF16))
        Abuf = ec(nc.sbuf_tensor("Abuf", [P, max(H, NHC_SLAB), max(TLOC, 2 * DW)], BF16))
        xbuf = ec(nc.sbuf_tensor("xbuf", [P, max(D, max(1, min(4, NTT // 2)) * DW)], F32))
        xn = None
        wbig = ec(nc.sbuf_tensor("wbig", [P, 2, D], BF16))
        kvstage = None
        kbuf = ec(nc.sbuf_tensor("kbuf", [P, 2, CH], BF16))
        vbuf = ec(nc.sbuf_tensor("vbuf", [P, 2, CH], BF16))
        pbuf = ec(nc.sbuf_tensor("pbuf", [P, 1, max(NKSUB * CH, 4 * TW)], BF16))
        tstage = ec(nc.sbuf_tensor("tstage", [P, 4, P], BF16))
        dmask = ec(nc.sbuf_tensor("dmask", [P, NKSUB, CH], BF16))
        ocol = ec(nc.sbuf_tensor("ocol", [P, H, P], BF16))
        selr = ec(nc.sbuf_tensor("selr", [H, H, P], BF16))
        onesH = ec(nc.sbuf_tensor("onesH", [H, P], F32))
        rowidH = ec(nc.sbuf_tensor("rowidH", [H, 1], I32))
        rowidHf = ec(nc.sbuf_tensor("rowidHf", [H, 1], F32))
        rowtmpH = ec(nc.sbuf_tensor("rowtmpH", [H, 1], F32))
        rcb = ec(nc.sbuf_tensor("rcb", [H, CH], BF16))
        bcb = ec(nc.sbuf_tensor("bcb", [P, CH], F32))
        relub = None
        amax = ec(nc.sbuf_tensor("amax", [P, NBQ], F32))
        qbase = ec(nc.sbuf_tensor("qbase", [P, NBQ], F32))
        qinvb = ec(nc.sbuf_tensor("qinvb", [P, NBQ], F32))
        qsse = ec(nc.sbuf_tensor("qsse", [P, NBQ], F32))
        qt1 = ec(nc.sbuf_tensor("qt1", [P, NBQ], F32))
        qbest = ec(nc.sbuf_tensor("qbest", [P, NBQ], F32))
        qmaskb = ec(nc.sbuf_tensor("qmaskb", [P, NBQ], F32))
        qbsc = ec(nc.sbuf_tensor("qbsc", [P, NBQ], F32))
        qscb = ec(nc.sbuf_tensor("qscb", [P, NBQ], F32))
        qscb2 = ec(nc.sbuf_tensor("qscb2", [P, NBQ], F32))
        qrb = ec(nc.sbuf_tensor("qrb", [P, NBQ], F32))
        qbh = ec(nc.sbuf_tensor("qbh", [P, NBQ], F32))
        qbl = ec(nc.sbuf_tensor("qbl", [P, NBQ], F32))
        qr8 = ec(nc.sbuf_tensor("qr8", [P, NBQ, 8], F32))
        qr4 = ec(nc.sbuf_tensor("qr4", [P, NBQ, 4], F32))
        lnm = ec(nc.sbuf_tensor("lnm", [P, 1], F32))
        lnv = ec(nc.sbuf_tensor("lnv", [P, 1], F32))
        lnr = ec(nc.sbuf_tensor("lnr", [P, 1], F32))
        b0 = ec(nc.sbuf_tensor("b0", [P, 1], F32))
        ident = ec(nc.sbuf_tensor("ident", [P, P], BF16))
        rowidf = ec(nc.sbuf_tensor("rowidf", [P, 1], F32))
        rowtmp = ec(nc.sbuf_tensor("rowtmp", [P, 1], F32))

        def arn(lo, n):
            return arena[:, lo:lo + n]
        AW0, AW1 = 0, QFDM
        AAW0 = 2 * QFDM
        AAW1 = 3 * QFDM
        AU = 4 * QFDM
        AQ = 5 * QFDM
        AY0 = 6 * QFDM
        AY1 = 7 * QFDM
        qint = arena[:, AQ:AQ + QFDM].bitcast(I32)
        deqv = arena[:, AU:AU + QFDM // 2].bitcast(BF16)
        colidf = arena[:, AAW1:AAW1 + max(CH, P)]
        colidi = arena[:, AAW0:AAW0 + max(CH, P)].bitcast(I32)
        lnsq = arena[:, AU:AU + D]

        psA = ec(nc.psum_tensor("psA", [P, 2 * max(CH, DW)], F32))
        psB = ec(nc.psum_tensor("psB", [P, 2 * max(CH, DW)], F32))
        psC = ec(nc.psum_tensor("psC", [P, max(CH, DW)], F32))
        psD = ec(nc.psum_tensor("psD", [P, max(CH, DW)], F32))
        psE = ec(nc.psum_tensor("psE", [P, max(CH, DW)], F32))
        psT0 = psA[:, 0:P // 2].bitcast(BF16)
        psT1 = psB[:, 0:P // 2].bitcast(BF16)

        kvstage = Abuf[:, 0:2, 0:DW].rearrange("p s c -> p s c")
        pbufS = [pbuf[:, 0, 0:NKSUB * CH],
                 xbuf[:, 0:(NKSUB * CH) // 2].bitcast(BF16)]
        xn = QT[:, 0:max(1, D // TLOC), 0:min(D, TLOC)].rearrange("p a b -> p (a b)")
        xn = xn.rearrange("p (o d) -> p o d", o=1)

        sem = {}
        for nm in ["vs", "ss", "ts", "gs",
                   "dx", "dla", "dlb", "dwa", "dsa", "dsb", "dka", "dkb",
                   "dfc", "dwp", "dqa", "dqb", "dqs", "dwl", "dwm", "dout",
                   "cw", "ckt", "cv", "cfc", "cwp"]:
            sem[nm] = ec(nc.semaphore(nm))
        block = ec(nc.Block())

        prog = {"sync": [], "vector": [], "scalar": [], "tensor": [], "gpsimd": []}

        def EM(eng, f, inc=None, n=1):
            if inc is None:
                prog[eng].append(f)
                return None
            k = _inc(inc, n)
            prog[eng].append(lambda e, f=f, s=sem[inc], n=n: f(e).then_inc(s, n))
            return k

        def WT(eng, key, val):
            prog[eng].append(lambda e, s=sem[key], v=val: e.wait_ge(s, v))

        V = lambda f: EM("vector", f, "vs")
        Sc = lambda f: EM("scalar", f, "ss")
        T = lambda f: EM("tensor", f, "ts")

        tr_state = {"seq": 0, "evac": {}}
        tstage_gate = [None]

        def emit_transpose(src_ap, dst_f):
            sq = tr_state["seq"]
            pv = psT0 if sq % 2 == 0 else psT1
            if sq >= 2:
                WT("tensor", "vs", tr_state["evac"][sq - 2])
            T(lambda t, src_ap=src_ap, pv=pv: t.transpose(pv, src_ap, ident[:]))
            ntr = _val("ts")
            WT("vector", "ts", ntr)
            V(lambda v, pv=pv, dst_f=dst_f: dst_f(v, pv))
            tr_state["evac"][sq] = _val("vs")
            tr_state["seq"] = sq + 1

        SA = cfg.get("stop_after", 99)

        class _Cut(Exception):
            pass

        def cut(k):
            if SA <= k:
                raise _Cut()

        try:
            EM("gpsimd", lambda g: g.iota(colidi[:, 0:max(CH, P)], pattern=[[1, max(CH, P)]],
                                          base=0, channel_multiplier=0), "gs")
            EM("gpsimd", lambda g: g.iota(qint[:, 0:1], pattern=[[0, 1]], base=0,
                                          channel_multiplier=1), "gs")
            EM("gpsimd", lambda g: g.iota(rowidH[:], pattern=[[0, 1]], base=0,
                                          channel_multiplier=1), "gs")
            n_giota = _val("gs")

            V(lambda v: v.memset(b0[:], 0.0))
            V(lambda v: v.memset(onesH[:], 1.0))
            WT("vector", "gs", n_giota)
            V(lambda v: v.tensor_copy(colidf[:, 0:max(CH, P)], colidi[:, 0:max(CH, P)]))
            V(lambda v: v.tensor_copy(rowidf[:], qint[:, 0:1]))
            V(lambda v: v.tensor_copy(rowidHf[:], rowidH[:]))
            n0 = _val("vs"); WT("vector", "vs", n0)
            V(lambda v: v.tensor_scalar(ident[:], colidf[:, 0:P], rowidf[:, 0:1], None, OP.is_equal))
            for j in range(NKSUB):
                if j > 0:
                    WT("vector", "vs", _val("vs"))
                V(lambda v, j=j: v.tensor_scalar(rowtmp[:], rowidf[:], float(P * j), None, OP.add))
                nrt = _val("vs"); WT("vector", "vs", nrt)
                V(lambda v, j=j: v.tensor_scalar(dmask[:, j, :], colidf[:, 0:CH], rowtmp[:, 0:1],
                                                 None, OP.is_ge))
            for h in range(H):
                V(lambda v, h=h: v.tensor_scalar(ocol[:, h, :], colidf[:, 0:P], float(h),
                                                 None, OP.is_equal))
            for h in range(H):
                if h > 0:
                    WT("vector", "vs", _val("vs"))
                V(lambda v, h=h: v.tensor_scalar(rowtmpH[:], rowidHf[:], float(h), None, OP.is_equal))
                nrt = _val("vs"); WT("vector", "vs", nrt)
                V(lambda v, h=h: v.tensor_scalar(selr[:, h, :], onesH[:], rowtmpH[:, 0:1],
                                                 None, OP.mult))
            n_consts = _val("vs")

            cut(0)
            wlist = [wq_d, wk_d, wv_d, wo_d]
            NW2 = max(1, DSL // P)
            PW = min(P, DSL)
            n_wcast = {}
            n_wtr = {}

            widx_r2 = [(w, r) for w in range(4) for r in range(NW2)]
            for ii, (widx, r2) in enumerate(widx_r2):
                slot = ii % 2
                wbv = arn(slot * QFDM, D)
                lsem = "dla" if slot == 0 else "dlb"
                if ii < 2:
                    WT("sync", "vs", n_consts)
                else:
                    WT("sync", "ss", n_wcast[ii - 2])
                EM("sync", lambda e, w=wlist[widx], r2=r2, wbv=wbv:
                   e.dma_start(out=wbv[0:PW, :], in_=w[r2 * P:r2 * P + PW, :]), lsem, 16)
                ndw = _val(lsem)
                WT("scalar", lsem, ndw)
                WT("scalar", "vs", n_consts)
                if ii > 0:
                    WT("scalar", "ts", n_wtr[ii - 1])
                Sc(lambda s, wbv=wbv: s.activation(
                    xn[0:PW, 0, :], wbv[0:PW, :], AF.Identity, bias=b0[0:PW, 0:1], scale=1.0))
                n_wcast[ii] = _val("ss")
                WT("tensor", "ss", n_wcast[ii])
                for g8 in range(0, NIC, 4):
                    if tstage_gate[0] is not None:
                        WT("vector", tstage_gate[0][0], tstage_gate[0][1])
                    for ic in range(g8, min(g8 + 4, NIC)):
                        def mkdst(ic=ic, g8=g8):
                            def dst(v, pv):
                                return v.tensor_copy(tstage[:, ic - g8, 0:PW], pv[0:P, 0:PW])
                            return dst
                        emit_transpose(xn[0:PW, 0, ic * P:ic * P + P], mkdst())
                    nst = _val("vs")
                    WT("sync", "vs", nst)
                    nch = min(4, NIC - g8)
                    EM("sync", lambda e, widx=widx, r2=r2, g8=g8, nch=nch:
                       e.dma_start(out=agw_in[widx * D + g8 * P: widx * D + (g8 + nch) * P,
                                              r2 * P:r2 * P + PW]
                                   .rearrange("(ic p) c -> p ic c", p=P),
                                   in_=tstage[:, 0:nch, 0:PW]), "dwa", 16)
                    tstage_gate[0] = ("dwa", _val("dwa"))
                n_wtr[ii] = _val("ts")
            WT("gpsimd", "dwa", _val("dwa"))
            EM("gpsimd", lambda g: g.collective_compute(
                "AllGather", OP.bypass, replica_groups=[list(range(NCORES))],
                ins=[agw_in.ap().opt()], outs=[agw_out.ap().opt()]), "cw")

            cut(1)
            ts_after_wprep = _val("ts")
            xsrc = [x0, x1]
            n_ln = [0] * NTT
            n_xnT = [0] * NTT
            for tt in range(NTT):
                src = xsrc[tt // (NTT // 2)]
                r0 = (tt % (NTT // 2)) * P
                if tt > 0:
                    WT("sync", "vs", n_ln[tt - 1])
                EM("sync", lambda e, src=src, r0=r0:
                   e.dma_start(out=xbuf[:, 0:D], in_=src[r0:r0 + P, :]), "dx", 16)
                ndx = _val("dx")
                cut(1.1)
                WT("vector", "dx", ndx)
                V(lambda v: v.tensor_reduce(lnm[:], xbuf[:, 0:D], AX.X, OP.add))
                cut(1.2)
                V(lambda v: v.scalar_tensor_tensor(lnsq, xbuf[:, 0:D], 1.0, xbuf[:, 0:D],
                                                   OP.mult, OP.mult, accum_out=lnv[:]))
                cut(1.3)
                n1 = _val("vs"); WT("vector", "vs", n1)
                V(lambda v: v.tensor_scalar(lnm[:], lnm[:], 1.0 / D, None, OP.mult))
                n2 = _val("vs"); WT("vector", "vs", n2)
                V(lambda v: v.tensor_tensor(lnr[:], lnm[:], lnm[:], OP.mult))
                n3 = _val("vs"); WT("vector", "vs", n3)
                V(lambda v: v.tensor_scalar(lnv[:], lnv[:], 1.0 / D, None, OP.mult))
                n4 = _val("vs"); WT("vector", "vs", n4)
                V(lambda v: v.tensor_tensor(lnv[:], lnv[:], lnr[:], OP.subtract))
                n5 = _val("vs"); WT("vector", "vs", n5)
                V(lambda v: v.tensor_scalar(lnv[:], lnv[:], LN_EPS, None, OP.add))
                n6 = _val("vs")
                WT("scalar", "vs", n6)
                Sc(lambda s: s.activation(lnr[:], lnv[:], AF.Ln, bias=b0[:, 0:1], scale=1.0))
                n7 = _val("ss"); WT("scalar", "ss", n7)
                cut(1.4)
                Sc(lambda s: s.activation(lnr[:], lnr[:], AF.Exp, bias=b0[:, 0:1], scale=-0.5))
                n8 = _val("ss")
                cut(1.5)
                WT("vector", "ss", n8)
                if tt >= 1:
                    WT("vector", "ts", n_xn_read[tt - 1])
                else:
                    WT("vector", "ts", ts_after_wprep)
                V(lambda v, tt=tt: v.tensor_scalar(xn[:, 0, :], xbuf[:, 0:D],
                                                   lnm[:, 0:1], lnr[:, 0:1], OP.subtract, OP.mult))
                n_ln[tt] = _val("vs")
                cut(1.6 + tt * 0.15)
                WT("tensor", "vs", n_ln[tt])
                for ic in range(NIC):
                    def mkdst(ic=ic, tt=tt):
                        def dst(v, pv):
                            return v.tensor_copy(xnT[:, ic, tt * P:(tt + 1) * P], pv[0:P, 0:P])
                        return dst
                    emit_transpose(xn[:, 0, ic * P:ic * P + P], mkdst())
                if tt == 0:
                    n_xn_read = {}
                n_xn_read[tt] = _val("ts")
                n_xnT[tt] = _val("vs")
                cut(1.7 + tt * 0.15)
            n_xnT_all = _val("vs")

            cut(2)
            li = [0]
            wl_use = {}

            gl_alt = [False]

            def gload(addr_list):
                s = li[0]
                eng = "sync" if (gl_alt[0] and s % 2 == 1) else "gpsimd"
                sm = ("dqa" if s % 2 == 0 else "dqb") if eng == "gpsimd" else "dqs"
                if s >= 2:
                    WT(eng, "ts", wl_use.get(s - 2, 0))
                for (dst, src_) in addr_list:
                    EM(eng, lambda g, dst=dst, src_=src_: g.dma_start(out=dst, in_=src_), sm, 16)
                li[0] = s + 1
                return (sm, _val(sm)), s

            WT("gpsimd", "cw", 1)
            stage_seq = [0]
            n_kstage = {}
            qk_evac = {}
            for widx, kind in ((0, "q"), (1, "k")):
                for h in range(H):
                    gi = widx * H + h
                    slot = li[0] % 2
                    r = (h * DK) // DSL
                    coff = (h * DK) % DSL
                    ndq, s = gload([(wbig[:, slot, :].rearrange("p (ic c) -> p ic c", c=DK),
                                     agw_out[r * 4 * D + widx * D: r * 4 * D + (widx + 1) * D,
                                             coff:coff + DK]
                                     .rearrange("(ic p) c -> p ic c", p=P))])
                    WT("tensor", ndq[0], ndq[1])
                    if gi == 0:
                        WT("tensor", "vs", n_xnT_all)
                    if gi >= 2:
                        WT("tensor", "ss", qk_evac[gi - 2])
                    pvb = psA if (h % 2 == 0) else psB
                    for ic in range(NIC):
                        for tw in range(NTW):
                            T(lambda t, pvb=pvb, slot=slot, ic=ic, tw=tw: t.matmul(
                                pvb[:, tw * TW:(tw + 1) * TW],
                                wbig[:, slot, ic * DK:(ic + 1) * DK],
                                xnT[:, ic, tw * TW:(tw + 1) * TW],
                                start=(ic == 0), stop=(ic == NIC - 1)))
                    wl_use[s] = _val("ts")
                    nmm = _val("ts")
                    WT("scalar", "ts", nmm)
                    for tw in range(NTW):
                        if kind == "q":
                            Sc(lambda s_, h=h, tw=tw, pvb=pvb: s_.activation(
                                QT[:, h, tw * TW:(tw + 1) * TW],
                                pvb[:, tw * TW:(tw + 1) * TW], AF.Identity,
                                bias=b0[:, 0:1], scale=1.0))
                        else:
                            kslot = stage_seq[0] % 2
                            ssem_ = "dsa" if kslot == 0 else "dsb"
                            if stage_seq[0] >= 2:
                                WT("scalar", n_kstage[stage_seq[0] - 2][0],
                                   n_kstage[stage_seq[0] - 2][1])
                            Sc(lambda s_, kslot=kslot, tw=tw, pvb=pvb: s_.activation(
                                kvstage[:, kslot, 0:TW], pvb[:, tw * TW:(tw + 1) * TW],
                                AF.Identity, bias=b0[:, 0:1], scale=1.0))
                            nev = _val("ss")
                            WT("sync", "ss", nev)
                            EM("sync", lambda e, h=h, tw=tw, kslot=kslot:
                               e.dma_start(out=agkt_in[h * DK:(h + 1) * DK, tw * TW:(tw + 1) * TW],
                                           in_=kvstage[0:DK, kslot, 0:TW]), ssem_, 16)
                            n_kstage[stage_seq[0]] = (ssem_, _val(ssem_))
                            stage_seq[0] += 1
                    qk_evac[gi] = _val("ss")
            TTGV = min(4, NTT)
            ICGV = max(1, min(4, D // DW))
            nseg = max(1, DW // DSL)
            segw = DW // nseg

            def pv4(tl):
                return (psA if (tl % 4) < 2 else psB)[:, (tl % 2) * DW:(tl % 2 + 1) * DW]

            v_evac = {}
            for pz in range(NTT // TTGV):
                for d2 in range(NDW):
                    gi = pz * NDW + d2
                    for icg in range(0, NIC, ICGV):
                        slot = li[0] % 2
                        lst = []
                        wv4 = wbig[:, slot, 0:ICGV * DW].rearrange("p (ic g c) -> p ic g c",
                                                                   g=nseg, c=segw)
                        for gsg in range(nseg):
                            c0 = d2 * DW + gsg * segw
                            r = c0 // DSL
                            coff = c0 % DSL
                            lst.append((wv4[:, :, gsg, :] if nseg > 1 else
                                        wbig[:, slot, :].rearrange("p (ic c) -> p ic c", c=segw),
                                        agw_out[r * 4 * D + 2 * D + icg * P:
                                                r * 4 * D + 2 * D + (icg + ICGV) * P,
                                                coff:coff + segw]
                                        .rearrange("(ic p) c -> p ic c", p=P)))
                        ndq, s = gload(lst)
                        WT("tensor", ndq[0], ndq[1])
                        if icg == 0:
                            WT("tensor", "ss", v_evac[gi - 1] if gi >= 1
                               else qk_evac[2 * H - 1])
                        for icl in range(ICGV):
                            for tl in range(TTGV):
                                tt = pz * TTGV + tl
                                T(lambda t, tl=tl, slot=slot, icl=icl, icg=icg, tt=tt: t.matmul(
                                    pv4(tl), xnT[:, icg + icl, tt * P:(tt + 1) * P],
                                    wbig[:, slot, icl * DW:(icl + 1) * DW],
                                    start=(icg == 0 and icl == 0),
                                    stop=(icg + ICGV == NIC and icl == ICGV - 1)))
                        wl_use[s] = _val("ts")
                    nmm = _val("ts")
                    WT("scalar", "ts", nmm)
                    for tl in range(TTGV):
                        tt = pz * TTGV + tl
                        kslot = stage_seq[0] % 2
                        ssem_ = "dsa" if kslot == 0 else "dsb"
                        if stage_seq[0] >= 2:
                            WT("scalar", n_kstage[stage_seq[0] - 2][0],
                               n_kstage[stage_seq[0] - 2][1])
                        Sc(lambda s_, kslot=kslot, tl=tl: s_.activation(
                            kvstage[:, kslot, 0:DW], pv4(tl), AF.Identity,
                            bias=b0[:, 0:1], scale=1.0))
                        nev = _val("ss")
                        WT("sync", "ss", nev)
                        EM("sync", lambda e, tt=tt, d2=d2, kslot=kslot:
                           e.dma_start(out=agv_in[tt * P:(tt + 1) * P, d2 * DW:(d2 + 1) * DW],
                                       in_=kvstage[:, kslot, 0:DW]), ssem_, 16)
                        n_kstage[stage_seq[0]] = (ssem_, _val(ssem_))
                        stage_seq[0] += 1
                    v_evac[gi] = _val("ss")
            WT("gpsimd", "dsa", _val("dsa"))
            WT("gpsimd", "dsb", _val("dsb"))
            EM("gpsimd", lambda g: g.collective_compute(
                "AllGather", OP.bypass, replica_groups=[list(range(NCORES))],
                ins=[agkt_in.ap().opt()], outs=[agkt_out.ap().opt()]), "ckt")
            EM("gpsimd", lambda g: g.collective_compute(
                "AllGather", OP.bypass, replica_groups=[list(range(NCORES))],
                ins=[agv_in.ap().opt()], outs=[agv_out.ap().opt()]), "cv")


            cut(3)
            n_qload = {}

            qg_cast = [0]
            qg_y1 = [0]
            qg_aw = {0: 0, 1: 0}
            qg_wb = {0: None, 1: None}
            vtrack = {"w": {}, "r": {}}

            def emit_quant(kind, idx, seq, qfd):
                nb = qfd // 64
                p = seq % 2
                wbk, awk = f"wb{p}", f"aw{p}"
                wbv = arn(p * QFDM, qfd)
                aw = arn((2 + p) * QFDM, qfd)
                uv = arn(AU, qfd)
                qiv = qint[:, 0:qfd]
                yS = [arn(AY0, qfd), arn(AY1, qfd)]
                yk = ["y0", "y1"]
                scbS = [qscb, qbh, qscb2]
                rS = [qinvb, qbl]
                src_t = wfc_d if kind == "fc" else wp_d
                lsem = "dla" if p == 0 else "dlb"
                VW, VR = vtrack["w"], vtrack["r"]

                def VO(f, reads=(), writes=(), ss=()):
                    need = 0
                    for k in reads:
                        need = max(need, VW.get(k, 0))
                    for k in writes:
                        need = max(need, VW.get(k, 0), VR.get(k, 0))
                    if need:
                        WT("vector", "vs", need)
                    for s_ in ss:
                        if s_:
                            WT("vector", "ss", s_)
                    kv = V(f)
                    for k in reads:
                        VR[k] = max(VR.get(k, 0), kv)
                    for k in writes:
                        VW[k] = kv
                    return kv

                if qg_wb[p] is not None:
                    WT("sync", "vs", max(qg_wb[p], VR.get(wbk, 0)))
                else:
                    WT("sync", "vs", n_xnT_all)
                    WT("sync", "ss", n_wcast[len(widx_r2) - 1])
                EM("sync", lambda e, src_t=src_t, idx=idx, wbv=wbv, qfd=qfd:
                   e.dma_start(out=wbv, in_=src_t[idx * P:(idx + 1) * P, 0:qfd]), lsem, 16)
                ndw = _val(lsem)

                R15 = float(np.float32(1.0) / np.float32(15.0))
                nbs = (P, nb, 64)
                r3 = lambda t: t.rearrange("p (s k) -> p s k", k=64)

                WT("scalar", lsem, ndw)
                WT("scalar", "vs", max(qg_aw[p], VR.get(awk, 0)) or n_consts)
                Sc(lambda s, wbv=wbv, aw=aw: s.activation(aw, wbv, AF.Abs,
                                                          bias=b0[:, 0:1], scale=1.0))
                nabs = _val("ss")

                WT("vector", lsem, ndw)
                VO(lambda v, wbv=wbv, nb=nb: v.tensor_reduce(
                    amax[:, 0:nb].rearrange("p (s o) -> p s o", o=1),
                    r3(wbv), AX.X, OP.max, apply_absolute_value=True),
                   reads=(wbk,), writes=("amax",))
                VO(lambda v, nb=nb: v.memset(qbest[:, 0:nb], 3.0e38), writes=("qbest",))
                VO(lambda v, nb=nb: v.tensor_scalar(qt1[:, 0:nb], amax[:, 0:nb],
                                                    1e-8, None, OP.max),
                   reads=("amax",), writes=("qt1",))
                VO(lambda v, nb=nb: v.memset(qbsc[:, 0:nb], 0.0), writes=("qbsc",))
                VO(lambda v, nb=nb: v.tensor_scalar(qbase[:, 0:nb], qt1[:, 0:nb],
                                                    R15, None, OP.mult),
                   reads=("qt1",), writes=("qbase",))

                cast_v = {}
                sq_v = {}

                def e_scb(j):
                    js = j % 3
                    VO(lambda v, nb=nb, js=js, s_j=SHRINKS[j]: v.tensor_scalar(
                        scbS[js][:, 0:nb], qbase[:, 0:nb], s_j, None, OP.mult),
                       reads=("qbase",), writes=(f"scb{js}",))

                def e_recip(j):
                    jp = j % 2
                    js = j % 3
                    VO(lambda v, nb=nb, jp=jp, js=js: v.reciprocal(rS[jp][:, 0:nb],
                                                                   scbS[js][:, 0:nb]),
                       reads=(f"scb{js}",), writes=(f"r{jp}",))

                u_v = {}

                def e_u(j):
                    jp = j % 2
                    u_v[j] = VO(lambda v, aw=aw, uv=uv, nb=nb, jp=jp: v.tensor_tensor(
                        r3(uv), r3(aw), rS[jp][:, 0:nb].broadcast_to(nbs), OP.mult),
                       reads=(awk, f"r{jp}"), writes=("u",),
                       ss=(qg_cast[0], nabs if j == 0 else 0))

                def e_cast(j):
                    WT("scalar", "vs", max(u_v[j], qg_y1[0]))
                    Sc(lambda s, uv=uv, qiv=qiv: s.activation(
                        qiv, uv, AF.Identity, bias=b0[:, 0:1], scale=1.0))
                    cast_v[j] = _val("ss")
                    qg_cast[0] = cast_v[j]

                def e_y1(j):
                    jp = j % 2
                    js = j % 3
                    VO(lambda v, qiv=qiv, nb=nb, jp=jp, js=js: v.scalar_tensor_tensor(
                        r3(yS[jp]), r3(qiv), QMAX,
                        scbS[js][:, 0:nb].broadcast_to(nbs), OP.min, OP.mult),
                       reads=("qint", f"scb{js}"), writes=(yk[jp],), ss=(cast_v[j],))
                    qg_y1[0] = _val("vs")

                def e_y2(j):
                    jp = j % 2
                    ny2 = VO(lambda v, aw=aw, jp=jp: v.tensor_tensor(
                        yS[jp], yS[jp], aw, OP.subtract),
                       reads=(yk[jp], awk), writes=(yk[jp],))
                    qg_aw[p] = ny2
                    WT("scalar", "vs", ny2)
                    Sc(lambda s, jp=jp: s.activation(yS[jp], yS[jp], AF.Square,
                                                     bias=b0[:, 0:1], scale=1.0))
                    sq_v[j] = _val("ss")

                def e_red(j, step):
                    jp = j % 2
                    if step == 0:
                        VO(lambda v, nb=nb, jp=jp: v.tensor_reduce(
                            qr8[:, 0:nb, :].rearrange("p s (k o) -> p s k o", o=1),
                            yS[jp].rearrange("p (s i k) -> p s k i", i=8, k=8), AX.X, OP.add),
                           reads=(yk[jp],), writes=("qr8",), ss=(sq_v[j],))
                    elif step == 1:
                        VO(lambda v, nb=nb: v.tensor_tensor(qr4[:, 0:nb, :], qr8[:, 0:nb, 0::2],
                                                            qr8[:, 0:nb, 1::2], OP.add),
                           reads=("qr8",), writes=("qr4",))
                    elif step == 2:
                        VO(lambda v, nb=nb: v.tensor_tensor(qr8[:, 0:nb, 0:2], qr4[:, 0:nb, 0::2],
                                                            qr4[:, 0:nb, 1::2], OP.add),
                           reads=("qr4",), writes=("qr8",))
                    elif step == 3:
                        VO(lambda v, nb=nb: v.tensor_tensor(
                            qsse[:, 0:nb].rearrange("p (s o) -> p s o", o=1),
                            qr8[:, 0:nb, 0:1], qr8[:, 0:nb, 1:2], OP.add),
                           reads=("qr8",), writes=("qsse",))
                    elif step == 4:
                        VO(lambda v, nb=nb: v.tensor_tensor(qmaskb[:, 0:nb], qsse[:, 0:nb],
                                                            qbest[:, 0:nb], OP.is_lt),
                           reads=("qsse", "qbest"), writes=("qmask",))
                    elif step == 5:
                        js = j % 3
                        VO(lambda v, nb=nb, js=js: v.tensor_tensor(qt1[:, 0:nb], scbS[js][:, 0:nb],
                                                                   qbsc[:, 0:nb], OP.subtract),
                           reads=(f"scb{js}", "qbsc"), writes=("qt1",))
                    elif step == 6:
                        VO(lambda v, nb=nb: v.tensor_tensor(qbest[:, 0:nb], qsse[:, 0:nb],
                                                            qbest[:, 0:nb], OP.min),
                           reads=("qsse", "qbest"), writes=("qbest",))
                    elif step == 7:
                        VO(lambda v, nb=nb: v.tensor_tensor(qt1[:, 0:nb], qt1[:, 0:nb],
                                                            qmaskb[:, 0:nb], OP.mult),
                           reads=("qt1", "qmask"), writes=("qt1",))
                    else:
                        VO(lambda v, nb=nb: v.tensor_tensor(qbsc[:, 0:nb], qbsc[:, 0:nb],
                                                            qt1[:, 0:nb], OP.add),
                           reads=("qbsc", "qt1"), writes=("qbsc",))

                e_scb(0)
                e_recip(0)
                for j in range(16):
                    e_u(j)
                    if j >= 2:
                        e_red(j - 2, 0)
                    if j >= 1:
                        e_y1(j - 1)
                    if j >= 2:
                        e_red(j - 2, 1)
                    e_cast(j)
                    if j >= 1:
                        e_y2(j - 1)
                    if j >= 2:
                        e_red(j - 2, 2)
                        e_red(j - 2, 3)
                        e_red(j - 2, 4)
                        e_red(j - 2, 5)
                    if j < 15:
                        e_scb(j + 1)
                    if j >= 2:
                        e_red(j - 2, 7)
                        e_red(j - 2, 6)
                    if j < 15:
                        e_recip(j + 1)
                    if j >= 2:
                        e_red(j - 2, 8)
                e_red(14, 0)
                e_y1(15)
                e_red(14, 1)
                e_y2(15)
                for st in (2, 3, 4, 5, 7, 6, 8):
                    e_red(14, st)
                for st in (0, 1, 2, 3, 4, 5, 7, 6, 8):
                    e_red(15, st)

                rbb = qrb[:, 0:nb].broadcast_to(nbs)
                bscb = qbsc[:, 0:nb].broadcast_to(nbs)
                VO(lambda v, nb=nb: v.reciprocal(qrb[:, 0:nb], qbsc[:, 0:nb]),
                   reads=("qbsc",), writes=("qrb",))
                ntf = VO(lambda v, wbv=wbv, uv=uv: v.tensor_tensor(r3(uv), r3(wbv), rbb, OP.mult),
                         reads=(wbk, "qrb"), writes=("u",), ss=(qg_cast[0],))
                qg_wb[p] = ntf
                ncl = VO(lambda v, uv=uv: v.tensor_scalar(uv, uv, CLIP_C, -CLIP_C, OP.min, OP.max),
                         reads=("u",), writes=("u",))
                WT("scalar", "vs", max(ncl, qg_y1[0]))
                Sc(lambda s, uv=uv, qiv=qiv: s.activation(
                    qiv, uv, AF.Identity, bias=b0[:, 0:1], scale=1.0))
                nqf = _val("ss")
                qg_cast[0] = nqf
                ndeq = VO(lambda v, nb=nb, qfd=qfd: v.tensor_tensor(
                    r3(deqv[:, 0:qfd]), r3(qint[:, 0:qfd]), bscb, OP.mult),
                    reads=("qint", "qbsc"), writes=("u",), ss=(nqf,))
                WT("tensor", "vs", ndeq)
                nchv = qfd // P
                for g8 in range(0, nchv, 4):
                    if tstage_gate[0] is not None:
                        WT("vector", tstage_gate[0][0], tstage_gate[0][1])
                    for ic in range(g8, min(g8 + 4, nchv)):
                        def mkdst(ic=ic, g8=g8):
                            def dst(v, pv):
                                return v.tensor_copy(tstage[:, ic - g8, :], pv[0:P, 0:P])
                            return dst
                        emit_transpose(deqv[:, ic * P:(ic + 1) * P], mkdst())
                    nstg = _val("vs")
                    WT("sync", "vs", nstg)
                    nch = min(4, nchv - g8)
                    dsem = "dfc" if kind == "fc" else "dwp"
                    dstT = agfc_in if kind == "fc" else agwp_in
                    EM("sync", lambda e, idx=idx, g8=g8, nch=nch, dstT=dstT:
                       e.dma_start(out=dstT[g8 * P:(g8 + nch) * P, idx * P:(idx + 1) * P]
                                   .rearrange("(ic p) c -> p ic c", p=P),
                                   in_=tstage[:, 0:nch, :]), dsem, 16)
                    tstage_gate[0] = (dsem, _val(dsem))
                VW["u"] = max(VW["u"], _val("vs"))
                n_qdone[seq] = _val("vs")

            n_qdone = {}
            seq = 0
            for idx in range(NQT_FC):
                emit_quant("fc", idx, seq, QFD_FC)
                seq += 1
            WT("gpsimd", "dfc", _val("dfc"))
            EM("gpsimd", lambda g: g.collective_compute(
                "AllGather", OP.bypass, replica_groups=[list(range(NCORES))],
                ins=[agfc_in.ap().opt()], outs=[agfc_out.ap().opt()]), "cfc")
            for idx in range(NQT_WP):
                emit_quant("wp", idx, seq, QFD_WP)
                seq += 1
            WT("gpsimd", "dwp", _val("dwp"))
            EM("gpsimd", lambda g: g.collective_compute(
                "AllGather", OP.bypass, replica_groups=[list(range(NCORES))],
                ins=[agwp_in.ap().opt()], outs=[agwp_out.ap().opt()]), "cwp")
            n_quant_v = _val("vs")
            if cfg.get("debug_deq"):
                WT("sync", "dfc", _val("dfc"))
                WT("sync", "dwp", _val("dwp"))
                EM("sync", lambda e: e.dma_start(out=dbg_fc, in_=agfc_in[:, :]), "dout", 16)
                EM("sync", lambda e: e.dma_start(out=dbg_wp, in_=agwp_in[:, :]), "dout", 16)

            cut(4)
            WT("sync", "ckt", 1)
            WT("sync", "cv", 1)
            WT("tensor", "vs", n_quant_v)
            arm_prog = {e: [[] for _ in range(NCORES)] for e in ("sync", "vector", "scalar", "tensor")}

            def build_arm(c, ap_loc):
                def AEM(eng, f, inc, n=1):
                    k = _inc(inc, n)
                    ap_loc[eng].append(lambda e, f=f, s=sem[inc], n=n: f(e).then_inc(s, n))
                    return k

                def AWT(eng, key, val):
                    ap_loc[eng].append(lambda e, s=sem[key], v=val: e.wait_ge(s, v))

                kv_seq = 0
                kv_use = {}
                pair_seq = 0
                pair_exp = {}
                for qc in range(2):
                    qpos = c if qc == 0 else (NCORES - 1 - c)
                    nblk = qpos + 1
                    cs_total = H * nblk * NKSUB
                    cs_i = 0
                    for h in range(H):
                        for kb in range(nblk):
                            slot = kv_seq % 2
                            if qc == 0:
                                krow = kb * D + h * DK
                                kcol = 0
                                vrow = kb * TLOC
                            else:
                                krow = (NCORES - 1 - kb) * D + h * DK
                                kcol = CH
                                vrow = (NCORES - 1 - kb) * TLOC + CH
                            if kv_seq >= 2:
                                AWT("sync", "ts", kv_use[kv_seq - 2])
                            ksem = "dka" if slot == 0 else "dkb"
                            AEM("sync", lambda e, krow=krow, kcol=kcol, slot=slot:
                                e.dma_start(out=kbuf[:, slot, :],
                                            in_=agkt_out[krow:krow + DK, kcol:kcol + CH]),
                                ksem, 16)
                            AEM("sync", lambda e, vrow=vrow, h=h, slot=slot:
                                e.dma_start(out=vbuf[:, slot, :].rearrange("p (j d) -> p j d", d=DK),
                                            in_=agv_out[vrow:vrow + CH, h * DK:(h + 1) * DK]
                                            .rearrange("(j p) d -> p j d", p=P)),
                                ksem, 16)
                            ndkv = _val(ksem)
                            diag = (kb == nblk - 1)
                            AWT("tensor", ksem, ndkv)
                            for jp in range(0, NKSUB, 2):
                                wpair = min(2, NKSUB - jp)
                                pps = psA if (pair_seq % 2 == 0) else psB
                                if pair_seq >= 2:
                                    AWT("tensor", "ss", pair_exp[pair_seq - 2])
                                for j in range(jp, jp + wpair):
                                    AEM("tensor", lambda t, pps=pps, j=j, jp=jp, slot=slot, h=h, qc=qc:
                                        t.matmul(pps[:, (j - jp) * CH:(j - jp + 1) * CH],
                                                 kbuf[:, slot, j * P:(j + 1) * P],
                                                 QT[:, h, qc * CH:(qc + 1) * CH],
                                                 start=True, stop=True), "ts")
                                nS = _val("ts")
                                AWT("scalar", "ts", nS)
                                if kv_seq >= 2 and jp == 0:
                                    AWT("scalar", "ts", kv_use[kv_seq - 2])
                                AEM("scalar", lambda s, pps=pps, jp=jp, wpair=wpair, slot=slot:
                                    s.activation(pbufS[slot][:, jp * CH:(jp + wpair) * CH],
                                                 pps[:, 0:wpair * CH], AF.Exp,
                                                 bias=b0[:, 0:1], scale=SQRT_DK_INV), "ss")
                                pair_exp[pair_seq] = _val("ss")
                                pair_seq += 1
                            nexp = _val("ss")
                            if diag:
                                AWT("vector", "ss", nexp)
                                for j in range(NKSUB):
                                    AEM("vector", lambda v, j=j, slot=slot:
                                        v.tensor_tensor(pbufS[slot][:, j * CH:(j + 1) * CH],
                                                        pbufS[slot][:, j * CH:(j + 1) * CH],
                                                        dmask[:, j, :], OP.mult), "vs")
                                npm = _val("vs")
                                AWT("tensor", "vs", npm)
                            else:
                                AWT("tensor", "ss", nexp)
                            for j in range(NKSUB):
                                AEM("tensor", lambda t, j=j, h=h, slot=slot,
                                    st_=(cs_i == 0), sp_=(cs_i == cs_total - 1):
                                    t.matmul(psD[:, 0:CH], ocol[:, h, :],
                                             pbufS[slot][:, j * CH:(j + 1) * CH],
                                             start=st_, stop=sp_), "ts")
                                cs_i += 1
                                AEM("tensor", lambda t, j=j, slot=slot,
                                    st_=(kb == 0 and j == 0),
                                    sp_=(kb == nblk - 1 and j == NKSUB - 1):
                                    t.matmul(psC[:, 0:CH], vbuf[:, slot, j * P:(j + 1) * P],
                                             pbufS[slot][:, j * CH:(j + 1) * CH],
                                             start=st_, stop=sp_), "ts")
                            kv_use[kv_seq] = _val("ts")
                            kv_seq += 1
                        nAV = _val("ts")
                        AWT("scalar", "ts", nAV)
                        AEM("scalar", lambda s, h=h, qc=qc: s.activation(
                            Abuf[:, h, qc * CH:(qc + 1) * CH], psC[:, 0:CH], AF.Identity,
                            bias=b0[:, 0:1], scale=1.0), "ss")
                        nevA = _val("ss")
                        AWT("tensor", "ss", nevA)
                    ncs = _val("ts")
                    AWT("vector", "ts", ncs)
                    AEM("vector", lambda v: v.tensor_copy(bcb[:, 0:CH], psD[:, 0:CH]), "vs")
                    nevc = _val("vs")
                    AWT("tensor", "vs", nevc)
                    AWT("vector", "vs", nevc)
                    AEM("vector", lambda v: v.reciprocal(bcb[0:H, 0:CH], bcb[0:H, 0:CH]), "vs")
                    nrc = _val("vs")
                    AWT("vector", "vs", nrc)
                    AEM("vector", lambda v: v.tensor_copy(rcb[:, 0:CH], bcb[0:H, 0:CH]), "vs")
                    nrcb = _val("vs")
                    last_ss = _val("ss")
                    for h in range(H):
                        AWT("tensor", "vs", nrcb)
                        AEM("tensor", lambda t, h=h: t.matmul(
                            psE[:, 0:CH], selr[:, h, :], rcb[:, 0:CH],
                            start=True, stop=True), "ts")
                        nbc = _val("ts")
                        AWT("vector", "ts", nbc)
                        AEM("vector", lambda v: v.tensor_copy(bcb[:, 0:CH], psE[:, 0:CH]), "vs")
                        nbe = _val("vs")
                        AWT("tensor", "vs", nbe)
                        AWT("vector", "vs", nbe)
                        AWT("vector", "ss", last_ss)
                        AEM("vector", lambda v, h=h, qc=qc: v.tensor_tensor(
                            Abuf[:, h, qc * CH:(qc + 1) * CH],
                            Abuf[:, h, qc * CH:(qc + 1) * CH], bcb[:, 0:CH], OP.mult), "vs")
                        nno = _val("vs")
                        AWT("vector", "vs", nno)

            snap0 = dict(ctr)
            deltas = None
            for c in range(NCORES):
                ctr.clear(); ctr.update(snap0)
                build_arm(c, {e: arm_prog[e][c] for e in arm_prog})
                d = {k: ctr[k] - snap0.get(k, 0) for k in ctr}
                d = {k: v for k, v in d.items() if v}
                if deltas is None:
                    deltas = d
                else:
                    assert d == deltas, f"arm {c}: {d} != {deltas}"

            for ename in ("sync", "vector", "scalar", "tensor"):
                def mk(ename=ename):
                    def run(e):
                        pid = e.partition_id()
                        for armc in e.Switch(pid, NCORES):
                            for f in arm_prog[ename][armc]:
                                f(e)
                    return run
                prog[ename].append(mk())
            n_attn_v = _val("vs")
            n_attn_t = _val("ts")
            n_attn_s = _val("ss")

            cut(5)
            gl_alt[0] = True
            WT("sync", "cw", 1)
            TTGO = max(1, min(4, NTT // 2))
            o_acc = {}
            for pz in range(NTT // TTGO):
                src_x = xsrc[(pz * TTGO) // (NTT // 2)]
                r0x = ((pz * TTGO) % (NTT // 2)) * P
                for d2 in range(NDW):
                    gi = pz * NDW + d2
                    if gi == 0:
                        WT("sync", "vs", n_quant_v)
                        WT("sync", "ts", n_attn_t)
                    else:
                        WT("sync", "vs", o_acc[gi - 1])
                    EM("sync", lambda e, src_x=src_x, r0x=r0x, d2=d2:
                       e.dma_start(out=xbuf[:, 0:TTGO * DW].rearrange("p (tl c) -> p tl c", c=DW),
                                   in_=src_x[r0x:r0x + TTGO * P, d2 * DW:(d2 + 1) * DW]
                                   .rearrange("(tl p) c -> p tl c", p=P)), "dx", 16)
                    ndx = _val("dx")
                    for kcg in range(0, H, ICGV):
                        slot = li[0] % 2
                        lst = []
                        wv4 = wbig[:, slot, 0:ICGV * DW].rearrange("p (kc g c) -> p kc g c",
                                                                   g=nseg, c=segw)
                        for gsg in range(nseg):
                            c0 = d2 * DW + gsg * segw
                            r = c0 // DSL
                            coff = c0 % DSL
                            lst.append((wv4[:, :, gsg, :] if nseg > 1 else
                                        wbig[:, slot, :].rearrange("p (kc c) -> p kc c", c=segw),
                                        agw_out[r * 4 * D + 3 * D + kcg * P:
                                                r * 4 * D + 3 * D + (kcg + ICGV) * P,
                                                coff:coff + segw]
                                        .rearrange("(kc p) c -> p kc c", p=P)))
                        ndq, s = gload(lst)
                        WT("tensor", ndq[0], ndq[1])
                        if gi == 0 and kcg == 0:
                            WT("tensor", "vs", n_attn_v)
                            WT("tensor", "vs", n_quant_v)
                        if kcg == 0 and gi >= 1:
                            WT("tensor", "vs", o_acc[gi - 1])
                        for kcl in range(ICGV):
                            for tl in range(TTGO):
                                tt = pz * TTGO + tl
                                T(lambda t, tl=tl, slot=slot, kcl=kcl, kcg=kcg, tt=tt: t.matmul(
                                    pv4(tl), Abuf[:, kcg + kcl, tt * P:(tt + 1) * P],
                                    wbig[:, slot, kcl * DW:(kcl + 1) * DW],
                                    start=(kcg == 0 and kcl == 0),
                                    stop=(kcg + ICGV == H and kcl == ICGV - 1)))
                        wl_use[s] = _val("ts")
                    nmm = _val("ts")
                    WT("vector", "ts", nmm)
                    WT("vector", "dx", ndx)
                    if gi == 0:
                        WT("vector", "vs", n_quant_v)
                    for tl in range(TTGO):
                        tt = pz * TTGO + tl
                        V(lambda v, tt=tt, d2=d2, tl=tl: v.tensor_tensor(
                            arena[:, tt * D + d2 * DW: tt * D + (d2 + 1) * DW],
                            xbuf[:, tl * DW:(tl + 1) * DW], pv4(tl), OP.add))
                    o_acc[gi] = _val("vs")
            n_oproj_t = _val("ts")

            cut(6)
            WT("gpsimd", "cfc", 1)
            WT("sync", "cfc", 1)
            pacc = {}
            wp_mm = {}
            fc_prev = [0]
            fc_sq = {}
            for sl in range(NSLAB):
                if sl >= 1:
                    WT("vector", "ts", wp_mm[(sl - 1) * NDW + NDW - 1])
                for hcl in range(NHC_SLAB):
                    hc = sl * NHC_SLAB + hcl
                    slot = li[0] % 2
                    r = (hc * P) // HIDC
                    coff = (hc * P) % HIDC
                    ndq, s = gload([(wbig[:, slot, :].rearrange("p (ic c) -> p ic c", c=P),
                                     agfc_out[r * D:(r + 1) * D, coff:coff + P]
                                     .rearrange("(ic p) c -> p ic c", p=P))])
                    WT("tensor", ndq[0], ndq[1])
                    if sl == 0 and hcl == 0:
                        WT("tensor", "vs", n_attn_v)
                    else:
                        WT("tensor", "ss", fc_prev[0])
                    for ic in range(NIC):
                        for tw in range(NTW):
                            T(lambda t, slot=slot, ic=ic, tw=tw: t.matmul(
                                (psC if tw == 0 else psD)[:, 0:TW],
                                wbig[:, slot, ic * P:(ic + 1) * P],
                                xnT[:, ic, tw * TW:(tw + 1) * TW],
                                start=(ic == 0), stop=(ic == NIC - 1)))
                    wl_use[s] = _val("ts")
                    nmm = _val("ts")
                    WT("scalar", "ts", nmm)
                    for tw in range(NTW):
                        rlv = pbuf[:, 0, 0:4 * TW].bitcast(F32)[:, tw * TW:(tw + 1) * TW]
                        if fc_sq.get(tw):
                            WT("scalar", "vs", fc_sq[tw])
                        Sc(lambda s_, tw=tw, rlv=rlv: s_.activation(
                            rlv, (psC if tw == 0 else psD)[:, 0:TW],
                            AF.Relu, bias=b0[:, 0:1], scale=1.0))
                        nrl = _val("ss")
                        WT("vector", "ss", nrl)
                        V(lambda v, hcl=hcl, tw=tw, rlv=rlv: v.tensor_tensor(
                            QT[:, hcl, tw * TW:(tw + 1) * TW], rlv, rlv, OP.mult))
                        fc_sq[tw] = _val("vs")
                    fc_prev[0] = _val("ss")
                n_gslab = _val("vs")
                if sl == 0:
                    WT("sync", "cwp", 1)
                    WT("sync", "ts", n_oproj_t)
                for d2 in range(NDW):
                    dp = d2 % 2
                    lin = sl * NDW + d2
                    wsem = "dwl" if dp == 0 else "dwm"
                    if lin >= 2:
                        WT("sync", "ts", wp_mm[lin - 2])
                    EM("sync", lambda e, sl=sl, d2=d2, dp=dp:
                       e.dma_start(out=Abuf[:, 0:NHC_SLAB, dp * DW:(dp + 1) * DW],
                                   in_=agwp_out[sl * NHC_SLAB * P:(sl + 1) * NHC_SLAB * P,
                                                d2 * DW:(d2 + 1) * DW]
                                   .rearrange("(hcl p) c -> p hcl c", p=P)), wsem, 16)
                    ndwl = _val(wsem)
                    WT("tensor", wsem, ndwl)
                    if d2 == 0:
                        WT("tensor", "vs", n_gslab)
                    for tt in range(NTT):
                        pv = (psA if (tt % 2 == 0) else psB)[:, dp * DW: dp * DW + DW]
                        bk = (tt % 2, dp)
                        if pacc.get(bk):
                            WT("tensor", "vs", pacc[bk])
                        for hcl in range(NHC_SLAB):
                            T(lambda t, pv=pv, hcl=hcl, tt=tt, dp=dp: t.matmul(
                                pv, QT[:, hcl, tt * P:(tt + 1) * P],
                                Abuf[:, hcl, dp * DW:(dp + 1) * DW],
                                start=(hcl == 0), stop=(hcl == NHC_SLAB - 1)))
                        nmm = _val("ts")
                        WT("vector", "ts", nmm)
                        V(lambda v, tt=tt, d2=d2, pv=pv: v.tensor_tensor(
                            arena[:, tt * D + d2 * DW: tt * D + (d2 + 1) * DW],
                            arena[:, tt * D + d2 * DW: tt * D + (d2 + 1) * DW], pv, OP.add))
                        pacc[bk] = _val("vs")
                    wp_mm[lin] = _val("ts")

            n_ffn_v = _val("vs")

            cut(7)
            WT("sync", "vs", n_ffn_v)
            for tt in range(NTT):
                dst = out0 if tt < NTT // 2 else out1
                r0 = (tt % (NTT // 2)) * P
                EM("sync", lambda e, dst=dst, r0=r0, tt=tt:
                   e.dma_start(out=dst[r0:r0 + P, :], in_=arena[:, tt * D:(tt + 1) * D]),
                   "dout", 16)
            WT("sync", "dout", _val("dout"))


        except _Cut:
            WT("sync", "vs", _val("vs"))
            WT("sync", "ss", _val("ss"))
            WT("sync", "ts", _val("ts"))
            WT("sync", "gs", _val("gs"))
            for tt in range(NTT):
                dst = out0 if tt < NTT // 2 else out1
                r0 = (tt % (NTT // 2)) * P
                EM("sync", lambda e, dst=dst, r0=r0, tt=tt:
                   e.dma_start(out=dst[r0:r0 + P, :], in_=arena[:, 0:D]), "dout", 16)
            WT("sync", "dout", _val("dout"))

        @block.sync
        def _(e):
            for f in prog["sync"]:
                f(e)

        @block.vector
        def _(e):
            for f in prog["vector"]:
                f(e)

        @block.scalar
        def _(e):
            for f in prog["scalar"]:
                f(e)

        @block.tensor
        def _(e):
            for f in prog["tensor"]:
                f(e)

        @block.gpsimd
        def _(e):
            for f in prog["gpsimd"]:
                f(e)

    return nc


_CACHE = {}


def kernel(L0_attn_in_features, L0_attn_q_proj, L0_attn_k_proj, L0_attn_v_proj,
           L0_attn_o_proj, L0_ffn_w_fc, L0_ffn_w_proj):
    cfg = REAL_CFG
    D = cfg["D"]; CH = cfg["CH"]; HID = cfg["HID"]
    HIDC = HID // NCORES
    DSL = D // NCORES
    x = np.ascontiguousarray(L0_attn_in_features)
    if "real" not in _CACHE:
        _CACHE["real"] = build_program(cfg)
    nc = _CACHE["real"]
    in_maps = []
    for c in range(NCORES):
        cc = NCORES - 1 - c
        in_maps.append({
            "x0": np.ascontiguousarray(x[0, c * CH:(c + 1) * CH, :]),
            "x1": np.ascontiguousarray(x[1, cc * CH:(cc + 1) * CH, :]),
            "wq": np.ascontiguousarray(L0_attn_q_proj[c * DSL:(c + 1) * DSL, :]),
            "wk": np.ascontiguousarray(L0_attn_k_proj[c * DSL:(c + 1) * DSL, :]),
            "wv": np.ascontiguousarray(L0_attn_v_proj[c * DSL:(c + 1) * DSL, :]),
            "wo": np.ascontiguousarray(L0_attn_o_proj[c * DSL:(c + 1) * DSL, :]),
            "wfc": np.ascontiguousarray(L0_ffn_w_fc[c * HIDC:(c + 1) * HIDC, :]),
            "wp": np.ascontiguousarray(L0_ffn_w_proj[:, c * HIDC:(c + 1) * HIDC]),
        })
    res = run_bass_kernel_spmd(nc, in_maps, core_ids=list(range(NCORES)))
    S = NCORES * CH
    out = np.empty((2, S, D), np.float32)
    for c in range(NCORES):
        cc = NCORES - 1 - c
        out[0, c * CH:(c + 1) * CH, :] = res.results[c]["out0"]
        out[1, cc * CH:(cc + 1) * CH, :] = res.results[c]["out1"]
    return out
